# version 03
# baseline (speedup 1.0000x reference)
import numpy as np
import ml_dtypes
import concourse.bass as bass
import concourse.mybir as mybir
from concourse.bass_utils import run_bass_kernel_spmd
from contextlib import ExitStack

F32 = mybir.dt.float32
BF16 = mybir.dt.bfloat16
AF = mybir.ActivationFunctionType
ALU = mybir.AluOpType
AX = mybir.AxisListType

T = 4096
CDIM = 1024
TT = 512
NT = T // TT
NB = T // 128
VW = 65
EPS = 1e-5

ENGINES = ("sync", "tensor", "scalar", "vector", "gpsimd")


class Sem:
    def __init__(self, h):
        self.h = h
        self.n = 0


class Plan:
    def __init__(self):
        self.ops = {e: [] for e in ENGINES}

    def w(self, e, sem, val):
        if val > 0:
            self.ops[e].append(lambda eng, s=sem.h, v=val: eng.wait_ge(s, v))

    def do(self, e, fn, sem=None, k=1):
        if sem is None:
            self.ops[e].append(fn)
            return None
        sem.n += k
        v = sem.n
        self.ops[e].append(
            lambda eng, f=fn, s=sem.h, kk=k: f(eng).then_inc(s, kk))
        return v

    def emit(self, nc):
        with nc.Block() as block:
            @block.sync
            def _(eng):
                for f in self.ops["sync"]:
                    f(eng)

            @block.tensor
            def _(eng):
                for f in self.ops["tensor"]:
                    f(eng)

            @block.scalar
            def _(eng):
                for f in self.ops["scalar"]:
                    f(eng)

            @block.vector
            def _(eng):
                for f in self.ops["vector"]:
                    f(eng)

            @block.gpsimd
            def _(eng):
                for f in self.ops["gpsimd"]:
                    f(eng)


def build_nc(debug=False):
    nc = bass.Bass("TRN2", target_bir_lowering=False, dynamic_dma_scratch_size=2048)

    x_h = nc.dram_tensor("x", [T, CDIM], F32, kind="ExternalInput").ap()
    wsrc = {}
    for m in ("wqT", "wkaT", "wvaT", "wo", "wrT", "wkfT", "wvfT"):
        wsrc[m] = nc.dram_tensor(m, [CDIM, CDIM], F32, kind="ExternalInput").ap()
    mur_h = nc.dram_tensor("mur", [128, 8], F32, kind="ExternalInput").ap()
    muk_h = nc.dram_tensor("muk", [128, 8], F32, kind="ExternalInput").ap()
    ident_h = nc.dram_tensor("ident", [128, 128], BF16, kind="ExternalInput").ap()
    maskw_h = nc.dram_tensor("maskw", [128, 896], BF16, kind="ExternalInput").ap()
    ones_h = nc.dram_tensor("ones64", [1, 64], F32, kind="ExternalInput").ap()

    out_h = nc.dram_tensor("out", [T, CDIM], F32, kind="ExternalOutput").ap()

    scr_kind = "ExternalOutput" if debug else "Internal"
    xnT_scr = nc.dram_tensor("xnT_scr", [CDIM, T], BF16, kind=scr_kind).ap()
    qT_scr = nc.dram_tensor("qT_scr", [CDIM, T], BF16, kind=scr_kind).ap()
    part_scr = nc.dram_tensor("part_scr", [T, CDIM], F32, kind=scr_kind).ap()

    with ExitStack() as top:
        sbt = lambda n, sh, dt: top.enter_context(nc.sbuf_tensor(n, sh, dt))
        sem = lambda n: Sem(top.enter_context(nc.semaphore(n)))

        d_const = sem("d_const")
        d_wm = {m: sem(f"d_w_{m}") for m in wsrc}
        d_x = [sem("d_x0"), sem("d_x1")]
        d_xn2 = sem("d_xn2")
        d_qt = sem("d_qt")
        d_pt = [sem(f"d_pt{i}") for i in range(3)]
        g_q = [sem("g_q0"), sem("g_q1")]
        g_xn = sem("g_xn")
        g_po = [sem("g_po0"), sem("g_po1")]
        g_out = [sem(f"g_out{i}") for i in range(3)]
        ts = sem("ts")
        ss = sem("ss")
        vs = sem("vs")

        ident = sbt("ident_s", [128, 128], BF16)
        maskw = sbt("maskw_s", [128, 896], BF16)
        ones64 = sbt("ones64_s", [1, 64], F32)
        mur = sbt("mur_s", [128, 8], F32)
        muk = sbt("muk_s", [128, 8], F32)
        epsb = sbt("epsb", [128, 1], F32)
        pv_scr = sbt("pv_scr", [128, 4], F32)
        ps_scr = sbt("ps_scr", [128, 4], F32)

        P1 = Plan()
        P1.do("sync", lambda e, d=ident[:], s=ident_h: e.dma_start(d, s), d_const, 16)
        P1.do("sync", lambda e, d=maskw[:], s=maskw_h: e.dma_start(d, s), d_const, 16)
        P1.do("sync", lambda e, d=ones64[:], s=ones_h: e.dma_start(d, s), d_const, 16)
        P1.do("sync", lambda e, d=mur[:], s=mur_h: e.dma_start(d, s), d_const, 16)
        P1.do("sync", lambda e, d=muk[:], s=muk_h: e.dma_start(d, s), d_const, 16)

        with ExitStack() as ph1:
            sb1 = lambda n, sh, dt: ph1.enter_context(nc.sbuf_tensor(n, sh, dt))
            wrTs = sb1("wrTs", [128, 8192], BF16)
            wkfTs = sb1("wkfTs", [128, 8192], BF16)
            wvfTs = sb1("wvfTs", [128, 8192], BF16)
            wqTs = sb1("wqTs", [128, 8192], BF16)
            xbuf = [sb1(f"xbuf{i}", [128, 4096], F32) for i in range(2)]
            xn_tok = sb1("xn_tok", [128, 4096], BF16)
            xnT = sb1("xnT", [128, 4096], BF16)
            dbufm = sb1("dbufm", [128, 512], F32)
            xrT = sb1("xrT", [128, 4096], BF16)
            xkT = sb1("xkT", [128, 4096], BF16)
            kactT = sb1("kactT", [128, 4096], BF16)
            rstore = sb1("rstore", [128, 4096], F32)
            scr4 = sb1("scr4", [128, 1024], F32)
            sqd = sb1("sqd", [128, 1024], F32)
            qe = [sb1(f"qe{i}", [128, 512], BF16) for i in range(2)]
            carry = sb1("carry", [128, 8], BF16)
            stat = sb1("stat", [128, 40], F32)
            stg = [sb1(f"stg{i}", [128, 1024], F32) for i in range(2)]

            SX, SXX, MEAN, EXX = slice(0, 4), slice(4, 8), slice(8, 12), slice(12, 16)
            M2, VAR, LNV, RSTD, NMR = (slice(16, 20), slice(20, 24), slice(24, 28),
                                       slice(28, 32), slice(32, 36))

            pA = ph1.enter_context(nc.psum_tensor("pA", [128, 2048], F32))
            pB = ph1.enter_context(nc.psum_tensor("pB", [128, 1024], F32))
            pC = ph1.enter_context(nc.psum_tensor("pC", [128, 2048], BF16))

            wdst1 = {"wrT": wrTs, "wkfT": wkfTs, "wvfT": wvfTs, "wqT": wqTs}
            stg_uses = [0, 0]
            wready = {}

            for name in ("wrT", "wkfT", "wvfT", "wqT"):
                for cc in range(8):
                    sl = cc % 2
                    P1.w("sync", vs, stg_uses[sl])
                    P1.do("sync", lambda e, d=stg[sl][:],
                          s=wsrc[name][128 * cc:128 * cc + 128, :]:
                          e.dma_start(d, s), d_wm[name], 16)
                    P1.w("vector", d_wm[name], 16 * (cc + 1))
                    stg_uses[sl] = P1.do(
                        "vector",
                        lambda e, d=wdst1[name][:, 1024 * cc:1024 * cc + 1024],
                        s=stg[sl][:]: e.tensor_copy(d, s), vs)
                wready[name] = stg_uses[(8 - 1) % 2]
            P1.do("vector", lambda e: e.memset(epsb[:], EPS), vs)
            P1.do("vector", lambda e: e.memset(carry[:], 0.0), vs)

            mix_done = {}
            evac_guard = {0: 0, 1: 0}
            pA_guard = {0: (ss, 0), 1: (ss, 0)}
            pB_guard = {0: (vs, 0), 1: (vs, 0)}
            last_ffn_vs = 0
            trans_done = {}
            qproj_done = {}
            g_q_uses = [0, 0]
            rsig_done = 0
            ke = 0

            for t in range(NT):
                xb = xbuf[t % 2]
                if t >= 2:
                    P1.w("sync", g_po[t % 2], 16 * (t // 2))
                P1.do("sync", lambda e, d=xb[:].rearrange("p (c f) -> p c f", c=4),
                      s=x_h[TT * t:TT * t + TT, :].rearrange("(c p) f -> p c f", p=128):
                      e.dma_start(d, s), d_x[t % 2], 16)
                dx_val = d_x[t % 2].n

                P1.w("vector", d_x[t % 2], dx_val)
                for tc in range(4):
                    P1.do("vector", lambda e, o=stat[:, SX][:, tc:tc + 1],
                          i=xb[:, 1024 * tc:1024 * tc + 1024]:
                          e.reduce_sum(o, i, axis=AX.X), vs)
                for tc in range(4):
                    P1.do("vector", lambda e, o=sqd[:],
                          i=xb[:, 1024 * tc:1024 * tc + 1024]:
                          e.tensor_mul(o, i, i), vs)
                    P1.do("vector", lambda e, o=stat[:, SXX][:, tc:tc + 1],
                          i=sqd[:]: e.reduce_sum(o, i, axis=AX.X), vs)
                P1.w("vector", vs, vs.n)
                mean_v = P1.do("vector", lambda e, o=stat[:, MEAN], i=stat[:, SX]:
                               e.tensor_scalar(o, i, 1.0 / CDIM, None, ALU.mult), vs)
                P1.do("vector", lambda e, o=stat[:, EXX], i=stat[:, SXX]:
                      e.tensor_scalar(o, i, 1.0 / CDIM, None, ALU.mult), vs)
                P1.w("vector", vs, mean_v)
                m2_v = P1.do("vector", lambda e, o=stat[:, M2], a=stat[:, MEAN]:
                             e.tensor_mul(o, a, a), vs)
                P1.w("vector", vs, m2_v)
                var_done = P1.do("vector", lambda e, o=stat[:, VAR], i=stat[:, M2],
                                 b=stat[:, EXX]:
                                 e.scalar_tensor_tensor(o, i, -1.0, b, ALU.mult,
                                                        ALU.add), vs)
                P1.w("vector", vs, var_done)
                varp = P1.do("vector", lambda e, o=pv_scr[:], i=stat[:, VAR]:
                             e.tensor_copy(o, i), vs)
                P1.w("scalar", vs, varp)
                P1.w("scalar", d_const, 80)
                lnv_v = P1.do("scalar", lambda e, o=stat[:, LNV], i=stat[:, VAR],
                              b=epsb[:]: e.activation(o, i, AF.Ln, bias=b), ss)
                P1.w("scalar", ss, lnv_v)
                rstd_done = P1.do("scalar", lambda e, o=stat[:, RSTD], i=stat[:, LNV]:
                                  e.activation(o, i, AF.Exp, scale=-0.5), ss)
                P1.w("scalar", ss, rstd_done)
                rstdp = P1.do("scalar", lambda e, o=ps_scr[:], i=stat[:, RSTD]:
                              e.activation(o, i, AF.Identity), ss)
                P1.w("vector", ss, rstdp)
                nmr_done = P1.do("vector", lambda e, o=stat[:, NMR], i=stat[:, MEAN],
                                 r=stat[:, RSTD]:
                                 e.scalar_tensor_tensor(o, i, -1.0, r, ALU.mult,
                                                        ALU.mult), vs)
                P1.w("vector", vs, nmr_done)
                nmrp = P1.do("vector", lambda e, o=pv_scr[:], i=stat[:, NMR]:
                             e.tensor_copy(o, i), vs)
                P1.w("scalar", vs, nmrp)
                if t >= 1:
                    P1.w("scalar", ts, trans_done[t - 1])
                xn_done = 0
                for tc in range(4):
                    xn_done = P1.do("scalar", lambda e,
                                    o=xn_tok[:, 1024 * tc:1024 * tc + 1024],
                                    i=xb[:, 1024 * tc:1024 * tc + 1024],
                                    b=stat[:, NMR][:, tc:tc + 1],
                                    s=stat[:, RSTD][:, tc:tc + 1]:
                                    e.activation(o, i, AF.Identity, bias=b, scale=s),
                                    ss)

                P1.w("tensor", ss, xn_done)
                if t == 0:
                    P1.w("tensor", d_const, 80)
                    P1.w("vector", d_const, 80)
                for cc in range(8):
                    half = cc % 2
                    P1.w("tensor", vs, evac_guard[half])
                    tdone = 0
                    for tc in range(4):
                        fn = lambda e, o=pC[:, 1024 * half + 128 * tc:
                                           1024 * half + 128 * tc + 128], \
                                    i=xn_tok[:, 1024 * tc + 128 * cc:
                                             1024 * tc + 128 * cc + 128], \
                                    idn=ident[:]: e.transpose(o, i, idn)
                        tdone = P1.do("tensor", fn, ts if tc == 3 else None)
                    if cc == 7:
                        trans_done[t] = tdone
                    P1.w("vector", ts, tdone)
                    if t >= 1 and cc == 0:
                        P1.w("vector", g_xn, 16 * t)
                        P1.w("vector", ts, qproj_done[t - 1])
                    ev = P1.do("vector", lambda e, o=xnT[:, 512 * cc:512 * cc + 512],
                               i=pC[:, 1024 * half:1024 * half + 512]:
                               e.tensor_copy(o, i), vs)
                    evac_guard[half] = ev
                    xsl = xnT[:, 512 * cc:512 * cc + 512]
                    P1.do("vector", lambda e, o=dbufm[:, 1:512], a=xsl[:, 1:512],
                          b=xsl[:, 0:511]: e.tensor_sub(o, a, b), vs)
                    P1.do("vector", lambda e, o=dbufm[:, 0:1], a=xsl[:, 0:1],
                          b=carry[:, cc:cc + 1]: e.tensor_sub(o, a, b), vs)
                    P1.do("vector", lambda e, o=xrT[:, 512 * cc + 1:512 * cc + 512],
                          i=dbufm[:, 1:512], m=mur[:, cc:cc + 1], b=xsl[:, 0:511]:
                          e.scalar_tensor_tensor(o, i, m, b, ALU.mult, ALU.add), vs)
                    P1.do("vector", lambda e, o=xrT[:, 512 * cc:512 * cc + 1],
                          i=dbufm[:, 0:1], m=mur[:, cc:cc + 1], b=carry[:, cc:cc + 1]:
                          e.scalar_tensor_tensor(o, i, m, b, ALU.mult, ALU.add), vs)
                    P1.do("vector", lambda e, o=xkT[:, 512 * cc + 1:512 * cc + 512],
                          i=dbufm[:, 1:512], m=muk[:, cc:cc + 1], b=xsl[:, 0:511]:
                          e.scalar_tensor_tensor(o, i, m, b, ALU.mult, ALU.add), vs)
                    md = P1.do("vector", lambda e, o=xkT[:, 512 * cc:512 * cc + 1],
                               i=dbufm[:, 0:1], m=muk[:, cc:cc + 1],
                               b=carry[:, cc:cc + 1]:
                               e.scalar_tensor_tensor(o, i, m, b, ALU.mult, ALU.add),
                               vs)
                    mix_done[cc] = md
                mix_last = P1.do(
                    "vector", lambda e, o=carry[:],
                    i=xnT[:].rearrange("p (c u) -> p c u", u=512)[:, :, 511:512]:
                    e.tensor_copy(o, i), vs)
                P1.w("scalar", vs, mix_last)
                P1.do("scalar", lambda e,
                      d=xnT_scr.rearrange("(cc p) u -> p cc u", p=128)[:, :, TT * t:TT * t + TT],
                      s=xnT[:].rearrange("p (cc u) -> p cc u", u=512):
                      e.dma_start(d, s), g_xn, 16)

                if t == 0:
                    P1.w("tensor", vs, wready["wrT"])
                for tc in range(4):
                    slot = tc % 2
                    gs, gv = pA_guard[slot]
                    P1.w("tensor", gs, gv)
                    rp = 0
                    for cin in range(8):
                        if tc == 0:
                            P1.w("tensor", vs, mix_done[cin])
                        for h in range(2):
                            fn = lambda e, o=pA[:, 1024 * slot + 512 * h:
                                               1024 * slot + 512 * h + 512], \
                                        l=xrT[:, 512 * cin + 128 * tc:
                                              512 * cin + 128 * tc + 128], \
                                        r=wrTs[:, 1024 * cin + 512 * h:
                                               1024 * cin + 512 * h + 512], \
                                        st=(cin == 0), sp=(cin == 7): \
                                e.matmul(o, l, r, start=st, stop=sp)
                            rp = P1.do("tensor", fn,
                                       ts if (cin == 7 and h == 1) else None)
                    P1.w("scalar", ts, rp)
                    if tc == 0:
                        P1.w("scalar", vs, last_ffn_vs)
                    else:
                        P1.w("scalar", vs, rsig_done)
                    rexp = P1.do("scalar", lambda e, o=scr4[:],
                                 i=pA[:, 1024 * slot:1024 * slot + 1024]:
                                 e.activation(o, i, AF.Exp, scale=-1.0), ss)
                    pA_guard[slot] = (ss, rexp)
                    P1.w("vector", ss, rexp)
                    P1.do("vector", lambda e, o=scr4[:]:
                          e.tensor_scalar(o, o, 1.0, None, ALU.add), vs)
                    rsig_done = P1.do(
                        "vector", lambda e, o=rstore[:, 1024 * tc:1024 * tc + 1024],
                        i=scr4[:]: e.reciprocal(o, i), vs)

                if t == 0:
                    P1.w("tensor", vs, wready["wkfT"])
                for co in range(8):
                    slot = co % 2
                    gs, gv = pB_guard[slot]
                    P1.w("tensor", gs, gv)
                    kp = 0
                    for cin in range(8):
                        fn = lambda e, o=pB[:, 512 * slot:512 * slot + 512], \
                                    l=wkfTs[:, 1024 * cin + 128 * co:
                                            1024 * cin + 128 * co + 128], \
                                    r=xkT[:, 512 * cin:512 * cin + 512], \
                                    st=(cin == 0), sp=(cin == 7): \
                            e.matmul(o, l, r, start=st, stop=sp)
                        kp = P1.do("tensor", fn, ts if cin == 7 else None)
                    P1.w("vector", ts, kp)
                    P1.w("vector", ss, rexp)
                    P1.do("vector", lambda e, o=scr4[:, 0:512],
                          i=pB[:, 512 * slot:512 * slot + 512]:
                          e.tensor_scalar(o, i, 0.0, None, ALU.max), vs)
                    ke = P1.do("vector", lambda e,
                               o=kactT[:, 512 * co:512 * co + 512],
                               a=scr4[:, 0:512]: e.tensor_mul(o, a, a), vs)
                    pB_guard[slot] = (vs, ke)

                if t == 0:
                    P1.w("tensor", vs, wready["wqT"])
                qp_last = 0
                for co in range(8):
                    slot = co % 2
                    gs, gv = pB_guard[slot]
                    P1.w("tensor", gs, gv)
                    qp = 0
                    for cin in range(8):
                        fn = lambda e, o=pB[:, 512 * slot:512 * slot + 512], \
                                    l=wqTs[:, 1024 * cin + 128 * co:
                                           1024 * cin + 128 * co + 128], \
                                    r=xnT[:, 512 * cin:512 * cin + 512], \
                                    st=(cin == 0), sp=(cin == 7): \
                            e.matmul(o, l, r, start=st, stop=sp)
                        qp = P1.do("tensor", fn, ts if cin == 7 else None)
                    qp_last = qp
                    P1.w("vector", ts, qp)
                    P1.w("vector", g_q[slot], g_q_uses[slot])
                    qev = P1.do("vector", lambda e, o=qe[slot][:],
                                i=pB[:, 512 * slot:512 * slot + 512]:
                                e.tensor_copy(o, i), vs)
                    pB_guard[slot] = (vs, qev)
                    P1.w("scalar", vs, qev)
                    P1.do("scalar", lambda e,
                          d=qT_scr[128 * co:128 * co + 128, TT * t:TT * t + TT],
                          s=qe[slot][:]: e.dma_start(d, s), g_q[slot], 16)
                    g_q_uses[slot] = g_q[slot].n
                qproj_done[t] = qp_last

                if t == 0:
                    P1.w("tensor", vs, wready["wvfT"])
                for tc in range(4):
                    slot = tc % 2
                    gs, gv = pA_guard[slot]
                    P1.w("tensor", gs, gv)
                    if tc == 0:
                        P1.w("tensor", vs, ke)
                    vp = 0
                    for din in range(8):
                        for h in range(2):
                            fn = lambda e, o=pA[:, 1024 * slot + 512 * h:
                                               1024 * slot + 512 * h + 512], \
                                        l=kactT[:, 512 * din + 128 * tc:
                                                512 * din + 128 * tc + 128], \
                                        r=wvfTs[:, 1024 * din + 512 * h:
                                                1024 * din + 512 * h + 512], \
                                        st=(din == 0), sp=(din == 7): \
                                e.matmul(o, l, r, start=st, stop=sp)
                            vp = P1.do("tensor", fn,
                                       ts if (din == 7 and h == 1) else None)
                    P1.w("vector", ts, vp)
                    P1.w("vector", ss, xn_done)
                    P1.do("vector", lambda e, o=scr4[:],
                          a=pA[:, 1024 * slot:1024 * slot + 1024],
                          b=rstore[:, 1024 * tc:1024 * tc + 1024]:
                          e.tensor_mul(o, a, b), vs)
                    ffn = P1.do("vector",
                                lambda e, o=xb[:, 1024 * tc:1024 * tc + 1024],
                                a=scr4[:], b=xb[:, 1024 * tc:1024 * tc + 1024]:
                                e.tensor_add(o, a, b), vs)
                    pA_guard[slot] = (vs, ffn)
                    last_ffn_vs = ffn
                P1.w("scalar", vs, last_ffn_vs)
                P1.do("scalar", lambda e,
                      d=part_scr[TT * t:TT * t + TT, :].rearrange("(c p) f -> p c f", p=128),
                      s=xb[:].rearrange("p (c f) -> p c f", c=4):
                      e.dma_start(d, s), g_po[t % 2], 16)

            ph1_ts, ph1_ss, ph1_vs = ts.n, ss.n, vs.n
            P1.emit(nc)

        P2 = Plan()
        with ExitStack() as ph2:
            sb2 = lambda n, sh, dt: ph2.enter_context(nc.sbuf_tensor(n, sh, dt))
            wkaTs = sb2("wkaTs", [128, 8192], BF16)
            wvaTs = sb2("wvaTs", [128, 8192], BF16)
            wos = sb2("wos", [128, 8192], BF16)
            Kst = sb2("Kst", [128, 32768], BF16)
            Vst = sb2("Vst", [128, NB * 1040], BF16)
            xn2 = sb2("xn2", [128, 4096], BF16)
            qt = sb2("qt", [128, 4096], BF16)
            ptb = sb2("ptb", [128, 2048], BF16)
            yt = sb2("yt", [128, 4096], BF16)
            pbuf = [sb2(f"pbuf{i}", [128, 512], F32) for i in range(3)]
            stg2 = [sb2("stg2_0", [128, 1024], F32)]
            linv = stg2[0][0:1, :]

            pS = ph2.enter_context(nc.psum_tensor("pS", [128, 2048], F32))
            pO = ph2.enter_context(nc.psum_tensor("pO", [128, 2048], F32))

            for e in ENGINES:
                P2.w(e, ts, ph1_ts)
                P2.w(e, ss, ph1_ss)
                P2.w(e, vs, ph1_vs)
                for s2 in g_po + g_q + [g_xn]:
                    P2.w(e, s2, s2.n)

            wdst2 = {"wkaT": wkaTs, "wvaT": wvaTs, "wo": wos}
            stg2_uses = [0, 0]
            w2ready = 0
            for name in ("wkaT", "wvaT", "wo"):
                for cc in range(8):
                    P2.w("sync", vs, w2ready)
                    P2.do("sync", lambda e, d=stg2[0][:],
                          s=wsrc[name][128 * cc:128 * cc + 128, :]:
                          e.dma_start(d, s), d_wm[name], 16)
                    P2.w("vector", d_wm[name], d_wm[name].n)
                    w2ready = P2.do(
                        "vector",
                        lambda e, d=wdst2[name][:, 1024 * cc:1024 * cc + 1024],
                        s=stg2[0][:]: e.tensor_copy(d, s), vs)
            vones = P2.do("vector", lambda e,
                          o=Vst[:].rearrange("p (x c) -> p x c", c=VW)[:, :, 64:65]:
                          e.memset(o, 1.0), vs)

            pS_guard = [(ss, 0), (ss, 0)]
            pO_vguard = {0: 0, 1: 0}
            vproj_done = {}
            qk_last = {}
            addc_prev = 0
            rr = [0]
            out_uses = [0, 0, 0]
            pidx = [0]

            for g in range(NT):
                if g >= 1:
                    P2.w("sync", ts, vproj_done[g - 1])
                P2.do("sync", lambda e, d=xn2[:].rearrange("p (cc u) -> p cc u", u=512),
                      s=xnT_scr.rearrange("(cc p) u -> p cc u", p=128)[:, :, TT * g:TT * g + TT]:
                      e.dma_start(d, s), d_xn2, 16)
                if g >= 1:
                    P2.w("sync", ts, qk_last[g - 1])
                P2.do("sync", lambda e, d=qt[:].rearrange("p (cc u) -> p cc u", u=512),
                      s=qT_scr.rearrange("(cc p) u -> p cc u", p=128)[:, :, TT * g:TT * g + TT]:
                      e.dma_start(d, s), d_qt, 16)

                P2.w("tensor", d_xn2, 16 * (g + 1))
                if g == 0:
                    P2.w("tensor", vs, w2ready)
                if g >= 1:
                    P2.w("tensor", vs, addc_prev)
                for pr in range(8):
                    half = pr % 2
                    gs, gv = pS_guard[half]
                    P2.w("tensor", gs, gv)
                    kp = 0
                    for cin in range(8):
                        fn = lambda e, o=pS[:, 1024 * half:1024 * half + 512], \
                                    l=wkaTs[:, 1024 * cin + 128 * pr:
                                            1024 * cin + 128 * pr + 128], \
                                    r=xn2[:, 512 * cin:512 * cin + 512], \
                                    st=(cin == 0), sp=(cin == 7): \
                            e.matmul(o, l, r, start=st, stop=sp)
                        kp = P2.do("tensor", fn, ts if cin == 7 else None)
                    P2.w("vector", ts, kp)
                    kev = P2.do("vector", lambda e,
                                o=Kst[:, 4096 * pr + TT * g:4096 * pr + TT * g + 512],
                                i=pS[:, 1024 * half:1024 * half + 512]:
                                e.tensor_copy(o, i), vs)
                    pS_guard[half] = (vs, kev)
                vev = 0
                for tkc in range(4):
                    half = tkc % 2
                    P2.w("tensor", vs, pO_vguard[half])
                    vp = 0
                    for cin in range(8):
                        for h in range(2):
                            fn = lambda e, o=pO[:, 1024 * half + 512 * h:
                                               1024 * half + 512 * h + 512], \
                                        l=xn2[:, 512 * cin + 128 * tkc:
                                              512 * cin + 128 * tkc + 128], \
                                        r=wvaTs[:, 1024 * cin + 512 * h:
                                                1024 * cin + 512 * h + 512], \
                                        st=(cin == 0), sp=(cin == 7): \
                                e.matmul(o, l, r, start=st, stop=sp)
                            vp = P2.do("tensor", fn,
                                       ts if (cin == 7 and h == 1) else None)
                    if tkc == 3:
                        vproj_done[g] = vp
                    P2.w("vector", ts, vp)
                    blk = 4 * g + tkc
                    vev = P2.do("vector", lambda e,
                                o=Vst[:, 1040 * blk:1040 * blk + 1040].rearrange(
                                    "p (h c) -> p h c", c=VW)[:, :, 0:64],
                                i=pO[:, 1024 * half:1024 * half + 1024].rearrange(
                                    "p (h c) -> p h c", c=64):
                                e.tensor_copy(o, i), vs)
                    pO_vguard[half] = vev
                kv_ev = vev

                J = 4 * g + 4
                P2.w("tensor", d_qt, 16 * (g + 1))
                P2.w("tensor", vs, kv_ev)
                qk_last_g = 0
                for p in range(8):
                    obase = 1024 * (p % 2)
                    expc = {}
                    maskc = {}
                    qkc = {}
                    slot_of = {}

                    def emit_pv(j, Jv=J, pv=p, ob=obase, ec=expc, mc=maskc,
                                so=slot_of):
                        if j in mc:
                            P2.w("tensor", vs, mc[j])
                        else:
                            P2.w("tensor", ss, ec[j])
                        if j == 0:
                            P2.w("tensor", vs, pO_vguard[pv % 2])
                        sl = so[j]
                        P2.do("tensor", lambda e,
                              o=pO[0:65, ob:ob + 512],
                              l=Vst[:, 1040 * j + VW * 2 * pv:
                                    1040 * j + VW * 2 * pv + VW],
                              r=ptb[:, 1024 * sl:1024 * sl + 512],
                              st=(j == 0), sp=(j == Jv - 1):
                              e.matmul(o, l, r, start=st, stop=sp,
                                       skip_group_check=True), None)
                        return P2.do("tensor", lambda e,
                                     o=pO[0:65, ob + 512:ob + 1024],
                                     l=Vst[:, 1040 * j + VW * (2 * pv + 1):
                                           1040 * j + VW * (2 * pv + 1) + VW],
                                     r=ptb[:, 1024 * sl + 512:1024 * sl + 1024],
                                     st=(j == 0), sp=(j == Jv - 1):
                                     e.matmul(o, l, r, start=st, stop=sp,
                                              skip_group_check=True),
                                     ts if j == Jv - 1 else None)

                    for j in range(J):
                        sl = rr[0] % 2
                        rr[0] += 1
                        slot_of[j] = sl
                        gs, gv = pS_guard[sl]
                        P2.w("tensor", gs, gv)
                        P2.do("tensor", lambda e,
                              o=pS[:, 1024 * sl:1024 * sl + 512],
                              l=Kst[0:64, 4096 * p + 128 * j:4096 * p + 128 * j + 128],
                              r=qt[0:64, 512 * p:512 * p + 512]:
                              e.matmul(o, l, r, start=True, stop=True,
                                       tile_position=(0, 0),
                                       skip_group_check=True), ts)
                        qkc[j] = P2.do("tensor", lambda e,
                                       o=pS[:, 1024 * sl + 512:1024 * sl + 1024],
                                       l=Kst[64:128, 4096 * p + 128 * j:
                                             4096 * p + 128 * j + 128],
                                       r=qt[64:128, 512 * p:512 * p + 512]:
                                       e.matmul(o, l, r, start=True, stop=True,
                                                tile_position=(64, 0),
                                                skip_group_check=True), ts)
                        P2.w("scalar", ts, qkc[j])
                        expc[j] = P2.do("scalar", lambda e,
                                        o=ptb[:, 1024 * sl:1024 * sl + 1024],
                                        i=pS[:, 1024 * sl:1024 * sl + 1024]:
                                        e.activation(o, i, AF.Exp, scale=0.125), ss)
                        pS_guard[sl] = (ss, expc[j])
                        if j >= 4 * g:
                            dlt = 128 * (j - 4 * g)
                            P2.w("vector", ss, expc[j])
                            P2.do("vector", lambda e,
                                  o=ptb[:, 1024 * sl:1024 * sl + 512],
                                  m=maskw[:, 384 - dlt:896 - dlt]:
                                  e.tensor_mul(o, o, m), vs)
                            maskc[j] = P2.do("vector", lambda e,
                                             o=ptb[:, 1024 * sl + 512:1024 * sl + 1024],
                                             m=maskw[:, 384 - dlt:896 - dlt]:
                                             e.tensor_mul(o, o, m), vs)
                        if j >= 1:
                            emit_pv(j - 1)
                    pvlast = emit_pv(J - 1)
                    qk_last_g = qkc[J - 1]

                    P2.w("vector", ts, pvlast)
                    P2.do("vector", lambda e, o=yt[0:64, 512 * p:512 * p + 512],
                          i=pO[0:64, obase:obase + 512]: e.tensor_copy(o, i), vs)
                    P2.do("vector", lambda e, o=yt[64:128, 512 * p:512 * p + 512],
                          i=pO[0:64, obase + 512:obase + 1024]:
                          e.tensor_copy(o, i), vs)
                    P2.do("vector", lambda e, o=linv[0:1, 0:512],
                          i=pO[64:65, obase:obase + 512]: e.reciprocal(o, i), vs)
                    rec0 = P2.do("vector", lambda e, o=linv[0:1, 512:1024],
                                 i=pO[64:65, obase + 512:obase + 1024]:
                                 e.reciprocal(o, i), vs)
                    P2.w("vector", vs, rec0)
                    rec = P2.do("vector", lambda e, o=pv_scr[0:1, 0:4],
                                i=linv[0:1, 1020:1024]: e.tensor_copy(o, i), vs)
                    sl2 = rr[0] % 2
                    rr[0] += 1
                    gs, gv = pS_guard[sl2]
                    P2.w("tensor", gs, gv)
                    P2.w("tensor", vs, rec)
                    P2.do("tensor", lambda e,
                          o=pS[0:64, 1024 * sl2:1024 * sl2 + 512],
                          l=ones64[0:1, :], r=linv[0:1, 0:512]:
                          e.matmul(o, l, r, start=True, stop=True,
                                   skip_group_check=True), ts)
                    r1 = P2.do("tensor", lambda e,
                               o=pS[64:128, 1024 * sl2 + 512:1024 * sl2 + 1024],
                               l=ones64[0:1, :], r=linv[0:1, 512:1024]:
                               e.matmul(o, l, r, start=True, stop=True,
                                        tile_position=(0, 64),
                                        skip_group_check=True), ts)
                    P2.w("vector", ts, r1)
                    P2.do("vector", lambda e, o=yt[0:64, 512 * p:512 * p + 512],
                          b=pS[0:64, 1024 * sl2:1024 * sl2 + 512]:
                          e.tensor_mul(o, o, b), vs)
                    yev = P2.do("vector", lambda e,
                                o=yt[64:128, 512 * p:512 * p + 512],
                                b=pS[64:128, 1024 * sl2 + 512:1024 * sl2 + 1024]:
                                e.tensor_mul(o, o, b), vs)
                    pS_guard[sl2] = (vs, yev)
                    pO_vguard[p % 2] = yev
                qk_last[g] = qk_last_g

                if g == 0:
                    P2.w("tensor", d_wm["wo"], d_wm["wo"].n)
                P2.w("tensor", vs, pO_vguard[0])
                P2.w("tensor", vs, pO_vguard[1])
                addc = {}
                for tqc in range(4):
                    off = 1024 * (tqc % 2)
                    if tqc >= 2:
                        P2.w("tensor", vs, addc[tqc - 2])
                    op = 0
                    for hp in range(8):
                        for h in range(2):
                            fn = lambda e, o=pO[:, off + 512 * h:off + 512 * h + 512], \
                                        l=yt[:, 512 * hp + 128 * tqc:
                                             512 * hp + 128 * tqc + 128], \
                                        r=wos[:, 1024 * hp + 512 * h:
                                              1024 * hp + 512 * h + 512], \
                                        st=(hp == 0), sp=(hp == 7): \
                                e.matmul(o, l, r, start=st, stop=sp,
                                         skip_group_check=True)
                            op = P2.do("tensor", fn,
                                       ts if (hp == 7 and h == 1) else None)
                    for h in range(2):
                        i = pidx[0]
                        pidx[0] += 1
                        sl = i % 3
                        P2.w("scalar", g_out[sl], out_uses[sl])
                        row = TT * g + 128 * tqc
                        P2.do("scalar", lambda e, d=pbuf[sl][:],
                              s=part_scr[row:row + 128, 512 * h:512 * h + 512]:
                              e.dma_start(d, s), d_pt[sl], 16)
                        pt_val = d_pt[sl].n
                        P2.w("vector", ts, op)
                        P2.w("vector", d_pt[sl], pt_val)
                        ad = P2.do("vector", lambda e, o=pbuf[sl][:],
                                   a=pO[:, off + 512 * h:off + 512 * h + 512],
                                   b=pbuf[sl][:]: e.tensor_add(o, a, b), vs)
                        addc[tqc] = ad
                        P2.w("scalar", vs, ad)
                        P2.do("scalar", lambda e,
                              d=out_h[row:row + 128, 512 * h:512 * h + 512],
                              s=pbuf[sl][:]: e.dma_start(d, s), g_out[sl], 16)
                        out_uses[sl] = g_out[sl].n
                addc_prev = addc[3]

            for sl in range(3):
                P2.w("scalar", g_out[sl], g_out[sl].n)

            P2.emit(nc)

    return nc


_NC_CACHE = None


def kernel(L0_attn_x_q, L0_attn_w_q, L0_attn_w_k, L0_attn_w_v, L0_attn_w_o,
           L0_ffn_W_r, L0_ffn_W_k, L0_ffn_W_v, L0_ffn_mu_r, L0_ffn_mu_k):
    global _NC_CACHE
    x = np.ascontiguousarray(np.asarray(L0_attn_x_q, dtype=np.float32))
    bf = ml_dtypes.bfloat16

    def tr(w):
        return np.ascontiguousarray(
            np.asarray(w, dtype=np.float32).reshape(CDIM, CDIM).T)

    common = dict(
        wqT=tr(L0_attn_w_q), wkaT=tr(L0_attn_w_k), wvaT=tr(L0_attn_w_v),
        wo=np.ascontiguousarray(np.asarray(L0_attn_w_o, dtype=np.float32)),
        wrT=tr(L0_ffn_W_r), wkfT=tr(L0_ffn_W_k), wvfT=tr(L0_ffn_W_v),
        mur=np.ascontiguousarray(
            np.asarray(L0_ffn_mu_r, dtype=np.float32).reshape(8, 128).T),
        muk=np.ascontiguousarray(
            np.asarray(L0_ffn_mu_k, dtype=np.float32).reshape(8, 128).T),
        ident=np.eye(128, dtype=bf),
        maskw=(np.arange(896)[None, :] >= (np.arange(128)[:, None] + 384)
               ).astype(bf),
        ones64=np.ones((1, 64), dtype=np.float32),
    )
    if _NC_CACHE is None:
        _NC_CACHE = build_nc()
    nc = _NC_CACHE
    in_maps = [dict(common, x=np.ascontiguousarray(x[i])) for i in range(8)]
    res = run_bass_kernel_spmd(nc, in_maps, core_ids=list(range(8)))
    return np.stack([r["out"] for r in res.results], axis=0)


if __name__ == "__main__":
    import reference as refmod
    inp = {k: np.asarray(v) for k, v in refmod.setup_inputs().items()}
    got = kernel(**inp)
    want = np.asarray(refmod.reference(**inp))
    denom = float(np.square(want).mean())
    print("resvar", float(np.square(got - want).mean() / denom),
          "maxabs", float(np.abs(got - want).max()))


# version 11
# speedup vs baseline: 1.0167x; 1.0167x over previous
import numpy as np
import ml_dtypes
import concourse.bass as bass
import concourse.mybir as mybir
from concourse.bass_utils import run_bass_kernel_spmd
from contextlib import ExitStack

F32 = mybir.dt.float32
BF16 = mybir.dt.bfloat16
AF = mybir.ActivationFunctionType
ALU = mybir.AluOpType
AX = mybir.AxisListType

T = 4096
CDIM = 1024
TT = 512
NT = T // TT
NB = T // 128
VW = 65
EPS = 1e-5

ENGINES = ("sync", "tensor", "scalar", "vector", "gpsimd")


class Sem:
    def __init__(self, h):
        self.h = h
        self.n = 0


class Plan:
    def __init__(self):
        self.ops = {e: [] for e in ENGINES}

    def w(self, e, sem, val):
        if val > 0:
            self.ops[e].append(lambda eng, s=sem.h, v=val: eng.wait_ge(s, v))

    def do(self, e, fn, sem=None, k=1):
        if sem is None:
            self.ops[e].append(fn)
            return None
        sem.n += k
        v = sem.n
        self.ops[e].append(
            lambda eng, f=fn, s=sem.h, kk=k: f(eng).then_inc(s, kk))
        return v

    def emit(self, nc):
        with nc.Block() as block:
            @block.sync
            def _(eng):
                for f in self.ops["sync"]:
                    f(eng)

            @block.tensor
            def _(eng):
                for f in self.ops["tensor"]:
                    f(eng)

            @block.scalar
            def _(eng):
                for f in self.ops["scalar"]:
                    f(eng)

            @block.vector
            def _(eng):
                for f in self.ops["vector"]:
                    f(eng)

            @block.gpsimd
            def _(eng):
                for f in self.ops["gpsimd"]:
                    f(eng)


def build_nc(debug=False):
    nc = bass.Bass("TRN2", target_bir_lowering=False, dynamic_dma_scratch_size=2048)

    x_h = nc.dram_tensor("x", [T, CDIM], F32, kind="ExternalInput").ap()
    wsrc = {}
    for m in ("wqT", "wkaT", "wvaT", "wo", "wrT", "wkfT", "wvfT"):
        wsrc[m] = nc.dram_tensor(m, [CDIM, CDIM], F32, kind="ExternalInput").ap()
    mur_h = nc.dram_tensor("mur", [128, 8], F32, kind="ExternalInput").ap()
    muk_h = nc.dram_tensor("muk", [128, 8], F32, kind="ExternalInput").ap()
    ident_h = nc.dram_tensor("ident", [128, 128], BF16, kind="ExternalInput").ap()
    maskd_h = nc.dram_tensor("maskd", [128, 2048], BF16, kind="ExternalInput").ap()
    ones_h = nc.dram_tensor("ones64", [1, 64], F32, kind="ExternalInput").ap()

    out_h = nc.dram_tensor("out", [T, CDIM], F32, kind="ExternalOutput").ap()

    scr_kind = "ExternalOutput" if debug else "Internal"
    xnT_scr = nc.dram_tensor("xnT_scr", [CDIM, T], BF16, kind=scr_kind).ap()
    qT_scr = nc.dram_tensor("qT_scr", [CDIM, T], BF16, kind=scr_kind).ap()
    part_scr = nc.dram_tensor("part_scr", [T, CDIM], F32, kind=scr_kind).ap()

    with ExitStack() as top:
        sbt = lambda n, sh, dt: top.enter_context(nc.sbuf_tensor(n, sh, dt))
        sem = lambda n: Sem(top.enter_context(nc.semaphore(n)))

        d_const = sem("d_const")
        d_wm = {m: sem(f"d_w_{m}") for m in wsrc}
        d_x = [sem("d_x0"), sem("d_x1")]
        d_xn2 = sem("d_xn2")
        d_qt = sem("d_qt")
        d_pt = [sem(f"d_pt{i}") for i in range(3)]
        g_q = [sem("g_q0"), sem("g_q1")]
        g_xn = sem("g_xn")
        g_po = [sem("g_po0"), sem("g_po1")]
        g_out = [sem(f"g_out{i}") for i in range(3)]
        ts = sem("ts")
        ss = sem("ss")
        vs = sem("vs")

        ident = sbt("ident_s", [128, 128], BF16)
        maskd = sbt("maskd_s", [128, 2048], BF16)
        ones64 = sbt("ones64_s", [1, 64], F32)
        mur = sbt("mur_s", [128, 8], F32)
        muk = sbt("muk_s", [128, 8], F32)
        epsb = sbt("epsb", [128, 1], F32)
        pv_scr = sbt("pv_scr", [128, 4], F32)
        ps_scr = sbt("ps_scr", [128, 4], F32)

        P1 = Plan()
        P1.do("sync", lambda e, d=ident[:], s=ident_h: e.dma_start(d, s), d_const, 16)
        P1.do("sync", lambda e, d=maskd[:], s=maskd_h: e.dma_start(d, s), d_const, 16)
        P1.do("sync", lambda e, d=ones64[:], s=ones_h: e.dma_start(d, s), d_const, 16)
        P1.do("sync", lambda e, d=mur[:], s=mur_h: e.dma_start(d, s), d_const, 16)
        P1.do("sync", lambda e, d=muk[:], s=muk_h: e.dma_start(d, s), d_const, 16)

        with ExitStack() as ph1:
            sb1 = lambda n, sh, dt: ph1.enter_context(nc.sbuf_tensor(n, sh, dt))
            wrTs = sb1("wrTs", [128, 8192], BF16)
            wkfTs = sb1("wkfTs", [128, 8192], BF16)
            wvfTs = sb1("wvfTs", [128, 8192], BF16)
            wqTs = sb1("wqTs", [128, 8192], BF16)
            xbuf = [sb1(f"xbuf{i}", [128, 4096], F32) for i in range(2)]
            xn_tok = sb1("xn_tok", [128, 4096], BF16)
            xnT = sb1("xnT", [128, 4096], BF16)
            dbufm = sb1("dbufm", [128, 512], F32)
            xrT = sb1("xrT", [128, 4096], BF16)
            xkT = sb1("xkT", [128, 4096], BF16)
            kactT = sb1("kactT", [128, 4096], BF16)
            rstore = sb1("rstore", [128, 4096], F32)
            scr4 = sb1("scr4", [128, 1024], F32)
            sqd = sb1("sqd", [128, 1024], F32)
            qe = [sb1(f"qe{i}", [128, 512], BF16) for i in range(2)]
            carry = sb1("carry", [128, 8], BF16)
            stat = sb1("stat", [128, 40], F32)
            stg = [sb1(f"stg{i}", [128, 1024], F32) for i in range(2)]

            SX, SXX, MEAN, EXX = slice(0, 4), slice(4, 8), slice(8, 12), slice(12, 16)
            M2, VAR, LNV, RSTD, NMR = (slice(16, 20), slice(20, 24), slice(24, 28),
                                       slice(28, 32), slice(32, 36))

            pA = ph1.enter_context(nc.psum_tensor("pA", [128, 2048], F32))
            pB = ph1.enter_context(nc.psum_tensor("pB", [128, 1024], F32))
            pC = ph1.enter_context(nc.psum_tensor("pC", [128, 2048], BF16))

            wdst1 = {"wrT": wrTs, "wkfT": wkfTs, "wvfT": wvfTs, "wqT": wqTs}
            stg_uses = [0, 0]
            wready = {}

            for name in ("wrT", "wkfT", "wvfT", "wqT"):
                for cc in range(8):
                    sl = cc % 2
                    P1.w("sync", vs, stg_uses[sl])
                    P1.do("sync", lambda e, d=stg[sl][:],
                          s=wsrc[name][128 * cc:128 * cc + 128, :]:
                          e.dma_start(d, s), d_wm[name], 16)
                    P1.w("vector", d_wm[name], 16 * (cc + 1))
                    stg_uses[sl] = P1.do(
                        "vector",
                        lambda e, d=wdst1[name][:, 1024 * cc:1024 * cc + 1024],
                        s=stg[sl][:]: e.tensor_copy(d, s), vs)
                wready[name] = stg_uses[(8 - 1) % 2]
            P1.do("vector", lambda e: e.memset(epsb[:], EPS), vs)
            P1.do("vector", lambda e: e.memset(carry[:], 0.0), vs)

            mix_done = {}
            evac_guard = {0: 0, 1: 0}
            pA_guard = {0: (ss, 0), 1: (ss, 0)}
            pB_guard = {0: (vs, 0), 1: (vs, 0)}
            last_ffn_vs = 0
            trans_done = {}
            qproj_done = {}
            g_q_uses = [0, 0]
            rsig_done = 0
            ke = 0

            for t in range(NT):
                xb = xbuf[t % 2]
                if t >= 2:
                    P1.w("sync", g_po[t % 2], 16 * (t // 2))
                P1.do("sync", lambda e, d=xb[:].rearrange("p (c f) -> p c f", c=4),
                      s=x_h[TT * t:TT * t + TT, :].rearrange("(c p) f -> p c f", p=128):
                      e.dma_start(d, s), d_x[t % 2], 16)
                dx_val = d_x[t % 2].n

                P1.w("vector", d_x[t % 2], dx_val)
                for tc in range(4):
                    P1.do("vector", lambda e, o=stat[:, SX][:, tc:tc + 1],
                          i=xb[:, 1024 * tc:1024 * tc + 1024]:
                          e.reduce_sum(o, i, axis=AX.X), vs)
                for tc in range(4):
                    P1.do("vector", lambda e, o=sqd[:],
                          i=xb[:, 1024 * tc:1024 * tc + 1024]:
                          e.tensor_mul(o, i, i), vs)
                    P1.do("vector", lambda e, o=stat[:, SXX][:, tc:tc + 1],
                          i=sqd[:]: e.reduce_sum(o, i, axis=AX.X), vs)
                P1.w("vector", vs, vs.n)
                mean_v = P1.do("vector", lambda e, o=stat[:, MEAN], i=stat[:, SX]:
                               e.tensor_scalar(o, i, 1.0 / CDIM, None, ALU.mult), vs)
                P1.do("vector", lambda e, o=stat[:, EXX], i=stat[:, SXX]:
                      e.tensor_scalar(o, i, 1.0 / CDIM, None, ALU.mult), vs)
                P1.w("vector", vs, mean_v)
                m2_v = P1.do("vector", lambda e, o=stat[:, M2], a=stat[:, MEAN]:
                             e.tensor_mul(o, a, a), vs)
                P1.w("vector", vs, m2_v)
                var_done = P1.do("vector", lambda e, o=stat[:, VAR], i=stat[:, M2],
                                 b=stat[:, EXX]:
                                 e.scalar_tensor_tensor(o, i, -1.0, b, ALU.mult,
                                                        ALU.add), vs)
                P1.w("vector", vs, var_done)
                varp = P1.do("vector", lambda e, o=pv_scr[:], i=stat[:, VAR]:
                             e.tensor_copy(o, i), vs)
                P1.w("scalar", vs, varp)
                P1.w("scalar", d_const, 80)
                lnv_v = P1.do("scalar", lambda e, o=stat[:, LNV], i=stat[:, VAR],
                              b=epsb[:]: e.activation(o, i, AF.Ln, bias=b), ss)
                P1.w("scalar", ss, lnv_v)
                rstd_done = P1.do("scalar", lambda e, o=stat[:, RSTD], i=stat[:, LNV]:
                                  e.activation(o, i, AF.Exp, scale=-0.5), ss)
                P1.w("scalar", ss, rstd_done)
                rstdp = P1.do("scalar", lambda e, o=ps_scr[:], i=stat[:, RSTD]:
                              e.activation(o, i, AF.Identity), ss)
                P1.w("vector", ss, rstdp)
                nmr_done = P1.do("vector", lambda e, o=stat[:, NMR], i=stat[:, MEAN],
                                 r=stat[:, RSTD]:
                                 e.scalar_tensor_tensor(o, i, -1.0, r, ALU.mult,
                                                        ALU.mult), vs)
                P1.w("vector", vs, nmr_done)
                nmrp = P1.do("vector", lambda e, o=pv_scr[:], i=stat[:, NMR]:
                             e.tensor_copy(o, i), vs)
                P1.w("scalar", vs, nmrp)
                if t >= 1:
                    P1.w("scalar", ts, trans_done[t - 1])
                xn_done = 0
                for tc in range(4):
                    xn_done = P1.do("scalar", lambda e,
                                    o=xn_tok[:, 1024 * tc:1024 * tc + 1024],
                                    i=xb[:, 1024 * tc:1024 * tc + 1024],
                                    b=stat[:, NMR][:, tc:tc + 1],
                                    s=stat[:, RSTD][:, tc:tc + 1]:
                                    e.activation(o, i, AF.Identity, bias=b, scale=s),
                                    ss)

                P1.w("tensor", ss, xn_done)
                if t == 0:
                    P1.w("tensor", d_const, 80)
                    P1.w("vector", d_const, 80)
                for cc in range(8):
                    half = cc % 2
                    P1.w("tensor", vs, evac_guard[half])
                    tdone = 0
                    for tc in range(4):
                        fn = lambda e, o=pC[:, 1024 * half + 128 * tc:
                                           1024 * half + 128 * tc + 128], \
                                    i=xn_tok[:, 1024 * tc + 128 * cc:
                                             1024 * tc + 128 * cc + 128], \
                                    idn=ident[:]: e.transpose(o, i, idn)
                        tdone = P1.do("tensor", fn, ts if tc == 3 else None)
                    if cc == 7:
                        trans_done[t] = tdone
                    P1.w("vector", ts, tdone)
                    if t >= 1 and cc == 0:
                        P1.w("vector", g_xn, 16 * t)
                        P1.w("vector", ts, qproj_done[t - 1])
                    ev = P1.do("vector", lambda e, o=xnT[:, 512 * cc:512 * cc + 512],
                               i=pC[:, 1024 * half:1024 * half + 512]:
                               e.tensor_copy(o, i), vs)
                    evac_guard[half] = ev
                    xsl = xnT[:, 512 * cc:512 * cc + 512]
                    P1.do("vector", lambda e, o=dbufm[:, 1:512], a=xsl[:, 1:512],
                          b=xsl[:, 0:511]: e.tensor_sub(o, a, b), vs)
                    P1.do("vector", lambda e, o=dbufm[:, 0:1], a=xsl[:, 0:1],
                          b=carry[:, cc:cc + 1]: e.tensor_sub(o, a, b), vs)
                    P1.do("vector", lambda e, o=xrT[:, 512 * cc + 1:512 * cc + 512],
                          i=dbufm[:, 1:512], m=mur[:, cc:cc + 1], b=xsl[:, 0:511]:
                          e.scalar_tensor_tensor(o, i, m, b, ALU.mult, ALU.add), vs)
                    P1.do("vector", lambda e, o=xrT[:, 512 * cc:512 * cc + 1],
                          i=dbufm[:, 0:1], m=mur[:, cc:cc + 1], b=carry[:, cc:cc + 1]:
                          e.scalar_tensor_tensor(o, i, m, b, ALU.mult, ALU.add), vs)
                    P1.do("vector", lambda e, o=xkT[:, 512 * cc + 1:512 * cc + 512],
                          i=dbufm[:, 1:512], m=muk[:, cc:cc + 1], b=xsl[:, 0:511]:
                          e.scalar_tensor_tensor(o, i, m, b, ALU.mult, ALU.add), vs)
                    md = P1.do("vector", lambda e, o=xkT[:, 512 * cc:512 * cc + 1],
                               i=dbufm[:, 0:1], m=muk[:, cc:cc + 1],
                               b=carry[:, cc:cc + 1]:
                               e.scalar_tensor_tensor(o, i, m, b, ALU.mult, ALU.add),
                               vs)
                    mix_done[cc] = md
                mix_last = P1.do(
                    "vector", lambda e, o=carry[:],
                    i=xnT[:].rearrange("p (c u) -> p c u", u=512)[:, :, 511:512]:
                    e.tensor_copy(o, i), vs)
                P1.w("scalar", vs, mix_last)
                P1.do("scalar", lambda e,
                      d=xnT_scr.rearrange("(cc p) u -> p cc u", p=128)[:, :, TT * t:TT * t + TT],
                      s=xnT[:].rearrange("p (cc u) -> p cc u", u=512):
                      e.dma_start(d, s), g_xn, 16)

                if t == 0:
                    P1.w("tensor", vs, wready["wrT"])
                for tc in range(4):
                    slot = tc % 2
                    gs, gv = pA_guard[slot]
                    P1.w("tensor", gs, gv)
                    rp = 0
                    for cin in range(8):
                        if tc == 0:
                            P1.w("tensor", vs, mix_done[cin])
                        for h in range(2):
                            fn = lambda e, o=pA[:, 1024 * slot + 512 * h:
                                               1024 * slot + 512 * h + 512], \
                                        l=xrT[:, 512 * cin + 128 * tc:
                                              512 * cin + 128 * tc + 128], \
                                        r=wrTs[:, 1024 * cin + 512 * h:
                                               1024 * cin + 512 * h + 512], \
                                        st=(cin == 0), sp=(cin == 7): \
                                e.matmul(o, l, r, start=st, stop=sp)
                            rp = P1.do("tensor", fn,
                                       ts if (cin == 7 and h == 1) else None)
                    P1.w("scalar", ts, rp)
                    if tc == 0:
                        P1.w("scalar", vs, last_ffn_vs)
                    else:
                        P1.w("scalar", vs, rsig_done)
                    rexp = P1.do("scalar", lambda e, o=scr4[:],
                                 i=pA[:, 1024 * slot:1024 * slot + 1024]:
                                 e.activation(o, i, AF.Exp, scale=-1.0), ss)
                    pA_guard[slot] = (ss, rexp)
                    P1.w("vector", ss, rexp)
                    P1.do("vector", lambda e, o=scr4[:]:
                          e.tensor_scalar(o, o, 1.0, None, ALU.add), vs)
                    rsig_done = P1.do(
                        "vector", lambda e, o=rstore[:, 1024 * tc:1024 * tc + 1024],
                        i=scr4[:]: e.reciprocal(o, i), vs)

                if t == 0:
                    P1.w("tensor", vs, wready["wkfT"])
                for co in range(8):
                    slot = co % 2
                    gs, gv = pB_guard[slot]
                    P1.w("tensor", gs, gv)
                    kp = 0
                    for cin in range(8):
                        fn = lambda e, o=pB[:, 512 * slot:512 * slot + 512], \
                                    l=wkfTs[:, 1024 * cin + 128 * co:
                                            1024 * cin + 128 * co + 128], \
                                    r=xkT[:, 512 * cin:512 * cin + 512], \
                                    st=(cin == 0), sp=(cin == 7): \
                            e.matmul(o, l, r, start=st, stop=sp)
                        kp = P1.do("tensor", fn, ts if cin == 7 else None)
                    P1.w("vector", ts, kp)
                    P1.w("vector", ss, rexp)
                    P1.do("vector", lambda e, o=scr4[:, 0:512],
                          i=pB[:, 512 * slot:512 * slot + 512]:
                          e.tensor_scalar(o, i, 0.0, None, ALU.max), vs)
                    ke = P1.do("vector", lambda e,
                               o=kactT[:, 512 * co:512 * co + 512],
                               a=scr4[:, 0:512]: e.tensor_mul(o, a, a), vs)
                    pB_guard[slot] = (vs, ke)

                if t == 0:
                    P1.w("tensor", vs, wready["wqT"])
                qp_last = 0
                for co in range(8):
                    slot = co % 2
                    gs, gv = pB_guard[slot]
                    P1.w("tensor", gs, gv)
                    qp = 0
                    for cin in range(8):
                        fn = lambda e, o=pB[:, 512 * slot:512 * slot + 512], \
                                    l=wqTs[:, 1024 * cin + 128 * co:
                                           1024 * cin + 128 * co + 128], \
                                    r=xnT[:, 512 * cin:512 * cin + 512], \
                                    st=(cin == 0), sp=(cin == 7): \
                            e.matmul(o, l, r, start=st, stop=sp)
                        qp = P1.do("tensor", fn, ts if cin == 7 else None)
                    qp_last = qp
                    P1.w("vector", ts, qp)
                    P1.w("vector", g_q[slot], g_q_uses[slot])
                    qev = P1.do("vector", lambda e, o=qe[slot][:],
                                i=pB[:, 512 * slot:512 * slot + 512]:
                                e.tensor_copy(o, i), vs)
                    pB_guard[slot] = (vs, qev)
                    P1.w("scalar", vs, qev)
                    P1.do("scalar", lambda e,
                          d=qT_scr[128 * co:128 * co + 128, TT * t:TT * t + TT],
                          s=qe[slot][:]: e.dma_start(d, s), g_q[slot], 16)
                    g_q_uses[slot] = g_q[slot].n
                qproj_done[t] = qp_last

                if t == 0:
                    P1.w("tensor", vs, wready["wvfT"])
                for tc in range(4):
                    slot = tc % 2
                    gs, gv = pA_guard[slot]
                    P1.w("tensor", gs, gv)
                    if tc == 0:
                        P1.w("tensor", vs, ke)
                    vp = 0
                    for din in range(8):
                        for h in range(2):
                            fn = lambda e, o=pA[:, 1024 * slot + 512 * h:
                                               1024 * slot + 512 * h + 512], \
                                        l=kactT[:, 512 * din + 128 * tc:
                                                512 * din + 128 * tc + 128], \
                                        r=wvfTs[:, 1024 * din + 512 * h:
                                                1024 * din + 512 * h + 512], \
                                        st=(din == 0), sp=(din == 7): \
                                e.matmul(o, l, r, start=st, stop=sp)
                            vp = P1.do("tensor", fn,
                                       ts if (din == 7 and h == 1) else None)
                    P1.w("vector", ts, vp)
                    P1.w("vector", ss, xn_done)
                    P1.do("vector", lambda e, o=scr4[:],
                          a=pA[:, 1024 * slot:1024 * slot + 1024],
                          b=rstore[:, 1024 * tc:1024 * tc + 1024]:
                          e.tensor_mul(o, a, b), vs)
                    ffn = P1.do("vector",
                                lambda e, o=xb[:, 1024 * tc:1024 * tc + 1024],
                                a=scr4[:], b=xb[:, 1024 * tc:1024 * tc + 1024]:
                                e.tensor_add(o, a, b), vs)
                    pA_guard[slot] = (vs, ffn)
                    last_ffn_vs = ffn
                P1.w("scalar", vs, last_ffn_vs)
                P1.do("scalar", lambda e,
                      d=part_scr[TT * t:TT * t + TT, :].rearrange("(c p) f -> p c f", p=128),
                      s=xb[:].rearrange("p (c f) -> p c f", c=4):
                      e.dma_start(d, s), g_po[t % 2], 16)

            ph1_ts, ph1_ss, ph1_vs = ts.n, ss.n, vs.n
            P1.emit(nc)

        P2 = Plan()
        with ExitStack() as ph2:
            sb2 = lambda n, sh, dt: ph2.enter_context(nc.sbuf_tensor(n, sh, dt))
            wkaTs = sb2("wkaTs", [128, 8192], BF16)
            wvaTs = sb2("wvaTs", [128, 8192], BF16)
            wos = sb2("wos", [128, 8192], BF16)
            Kst = sb2("Kst", [128, 32768], BF16)
            Vst = sb2("Vst", [128, NB * 1040], BF16)
            xn2 = sb2("xn2", [128, 4096], BF16)
            qt = sb2("qt", [128, 4096], BF16)
            ptb = sb2("ptb", [128, 2048], BF16)
            yt = sb2("yt", [128, 4096], BF16)
            pbuf = [sb2(f"pbuf{i}", [128, 512], F32) for i in range(3)]
            stg2 = [sb2("stg2_0", [128, 1024], F32)]
            linv = stg2[0][0:1, :]

            pS = ph2.enter_context(nc.psum_tensor("pS", [128, 2048], F32))
            pO = ph2.enter_context(nc.psum_tensor("pO", [128, 2048], F32))

            for e in ENGINES:
                P2.w(e, ts, ph1_ts)
                P2.w(e, ss, ph1_ss)
                P2.w(e, vs, ph1_vs)
                for s2 in g_po + g_q + [g_xn]:
                    P2.w(e, s2, s2.n)

            wdst2 = {"wkaT": wkaTs, "wvaT": wvaTs, "wo": wos}
            stg2_uses = [0, 0]
            w2ready = 0
            for name in ("wkaT", "wvaT", "wo"):
                for cc in range(8):
                    P2.w("sync", vs, w2ready)
                    P2.do("sync", lambda e, d=stg2[0][:],
                          s=wsrc[name][128 * cc:128 * cc + 128, :]:
                          e.dma_start(d, s), d_wm[name], 16)
                    P2.w("vector", d_wm[name], d_wm[name].n)
                    w2ready = P2.do(
                        "vector",
                        lambda e, d=wdst2[name][:, 1024 * cc:1024 * cc + 1024],
                        s=stg2[0][:]: e.tensor_copy(d, s), vs)
            vones = P2.do("vector", lambda e,
                          o=Vst[:].rearrange("p (x c) -> p x c", c=VW)[:, :, 64:65]:
                          e.memset(o, 1.0), vs)

            pS_guard = [(ss, 0), (ss, 0)]
            pO_vguard = {0: 0, 1: 0}
            vproj_done = {}
            qk_last = {}
            addc_prev = 0
            rr = [0]
            out_uses = [0, 0, 0]
            pidx = [0]

            for g in range(NT):
                if g >= 1:
                    P2.w("sync", ts, vproj_done[g - 1])
                P2.do("sync", lambda e, d=xn2[:].rearrange("p (cc u) -> p cc u", u=512),
                      s=xnT_scr.rearrange("(cc p) u -> p cc u", p=128)[:, :, TT * g:TT * g + TT]:
                      e.dma_start(d, s), d_xn2, 16)
                if g >= 1:
                    P2.w("sync", ts, qk_last[g - 1])
                P2.do("sync", lambda e, d=qt[:].rearrange("p (cc u) -> p cc u", u=512),
                      s=qT_scr.rearrange("(cc p) u -> p cc u", p=128)[:, :, TT * g:TT * g + TT]:
                      e.dma_start(d, s), d_qt, 16)

                P2.w("tensor", d_xn2, 16 * (g + 1))
                if g == 0:
                    P2.w("tensor", vs, w2ready)
                if g >= 1:
                    P2.w("tensor", vs, addc_prev)
                for pr in range(8):
                    half = pr % 2
                    gs, gv = pS_guard[half]
                    P2.w("tensor", gs, gv)
                    kp = 0
                    for cin in range(8):
                        fn = lambda e, o=pS[:, 1024 * half:1024 * half + 512], \
                                    l=wkaTs[:, 1024 * cin + 128 * pr:
                                            1024 * cin + 128 * pr + 128], \
                                    r=xn2[:, 512 * cin:512 * cin + 512], \
                                    st=(cin == 0), sp=(cin == 7): \
                            e.matmul(o, l, r, start=st, stop=sp)
                        kp = P2.do("tensor", fn, ts if cin == 7 else None)
                    P2.w("vector", ts, kp)
                    kev = P2.do("vector", lambda e,
                                o=Kst[:, 4096 * pr + TT * g:4096 * pr + TT * g + 512],
                                i=pS[:, 1024 * half:1024 * half + 512]:
                                e.tensor_copy(o, i), vs)
                    pS_guard[half] = (vs, kev)
                vev = 0
                for tkc in range(4):
                    half = tkc % 2
                    P2.w("tensor", vs, pO_vguard[half])
                    vp = 0
                    for cin in range(8):
                        for h in range(2):
                            fn = lambda e, o=pO[:, 1024 * half + 512 * h:
                                               1024 * half + 512 * h + 512], \
                                        l=xn2[:, 512 * cin + 128 * tkc:
                                              512 * cin + 128 * tkc + 128], \
                                        r=wvaTs[:, 1024 * cin + 512 * h:
                                                1024 * cin + 512 * h + 512], \
                                        st=(cin == 0), sp=(cin == 7): \
                                e.matmul(o, l, r, start=st, stop=sp)
                            vp = P2.do("tensor", fn,
                                       ts if (cin == 7 and h == 1) else None)
                    if tkc == 3:
                        vproj_done[g] = vp
                    P2.w("vector", ts, vp)
                    blk = 4 * g + tkc
                    vev = P2.do("vector", lambda e,
                                o=Vst[:, 1040 * blk:1040 * blk + 1040].rearrange(
                                    "p (h c) -> p h c", c=VW)[:, :, 0:64],
                                i=pO[:, 1024 * half:1024 * half + 1024].rearrange(
                                    "p (h c) -> p h c", c=64):
                                e.tensor_copy(o, i), vs)
                    pO_vguard[half] = vev
                kv_ev = vev

                J = 4 * g + 4
                P2.w("tensor", d_qt, 16 * (g + 1))
                P2.w("tensor", vs, kv_ev)
                qk_last_g = 0
                for p in range(8):
                    obase = 1024 * (p % 2)
                    expc = {}
                    qkc = {}
                    slot_of = {}

                    def emit_pv(j, Jv=J, pv=p, ob=obase, ec=expc,
                                so=slot_of):
                        P2.w("tensor", ss, ec[j])
                        if j == 0:
                            P2.w("tensor", vs, pO_vguard[pv % 2])
                        sl = so[j]
                        P2.do("tensor", lambda e,
                              o=pO[0:65, ob:ob + 512],
                              l=Vst[:, 1040 * j + VW * 2 * pv:
                                    1040 * j + VW * 2 * pv + VW],
                              r=ptb[:, 1024 * sl:1024 * sl + 512],
                              st=(j == 0), sp=(j == Jv - 1):
                              e.matmul(o, l, r, start=st, stop=sp,
                                       skip_group_check=True), None)
                        return P2.do("tensor", lambda e,
                                     o=pO[0:65, ob + 512:ob + 1024],
                                     l=Vst[:, 1040 * j + VW * (2 * pv + 1):
                                           1040 * j + VW * (2 * pv + 1) + VW],
                                     r=ptb[:, 1024 * sl + 512:1024 * sl + 1024],
                                     st=(j == 0), sp=(j == Jv - 1):
                                     e.matmul(o, l, r, start=st, stop=sp,
                                              skip_group_check=True),
                                     ts if j == Jv - 1 else None)

                    for j in range(J):
                        sl = rr[0] % 2
                        rr[0] += 1
                        slot_of[j] = sl
                        diag = j >= 4 * g
                        gs, gv = pS_guard[sl]
                        P2.w("tensor", gs, gv)
                        P2.do("tensor", lambda e,
                              o=pS[:, 1024 * sl:1024 * sl + 512],
                              l=Kst[0:64, 4096 * p + 128 * j:4096 * p + 128 * j + 128],
                              r=qt[0:64, 512 * p:512 * p + 512], sp=(not diag):
                              e.matmul(o, l, r, start=True, stop=sp,
                                       tile_position=(0, 0),
                                       skip_group_check=True), ts)
                        qkc[j] = P2.do("tensor", lambda e,
                                       o=pS[:, 1024 * sl + 512:1024 * sl + 1024],
                                       l=Kst[64:128, 4096 * p + 128 * j:
                                             4096 * p + 128 * j + 128],
                                       r=qt[64:128, 512 * p:512 * p + 512],
                                       sp=(not diag):
                                       e.matmul(o, l, r, start=True, stop=sp,
                                                tile_position=(64, 0),
                                                skip_group_check=True), ts)
                        if diag:
                            P2.do("tensor", lambda e,
                                  o=pS[:, 1024 * sl:1024 * sl + 512],
                                  l=ident[:],
                                  r=maskd[:, 512 * (j - 4 * g):
                                          512 * (j - 4 * g) + 512]:
                                  e.matmul(o, l, r, start=False, stop=True,
                                           skip_group_check=True), ts)
                            qkc[j] = P2.do("tensor", lambda e,
                                           o=pS[:, 1024 * sl + 512:1024 * sl + 1024],
                                           l=ident[:],
                                           r=maskd[:, 512 * (j - 4 * g):
                                                   512 * (j - 4 * g) + 512]:
                                           e.matmul(o, l, r, start=False, stop=True,
                                                    skip_group_check=True), ts)
                        P2.w("scalar", ts, qkc[j])
                        expc[j] = P2.do("scalar", lambda e,
                                        o=ptb[:, 1024 * sl:1024 * sl + 1024],
                                        i=pS[:, 1024 * sl:1024 * sl + 1024]:
                                        e.activation(o, i, AF.Exp, scale=0.125), ss)
                        pS_guard[sl] = (ss, expc[j])
                        if j >= 1:
                            emit_pv(j - 1)
                    pvlast = emit_pv(J - 1)
                    qk_last_g = qkc[J - 1]

                    P2.w("vector", ts, pvlast)
                    P2.do("vector", lambda e, o=yt[0:64, 512 * p:512 * p + 512],
                          i=pO[0:64, obase:obase + 512]: e.tensor_copy(o, i), vs)
                    P2.do("vector", lambda e, o=yt[64:128, 512 * p:512 * p + 512],
                          i=pO[0:64, obase + 512:obase + 1024]:
                          e.tensor_copy(o, i), vs)
                    P2.do("vector", lambda e, o=linv[0:1, 0:512],
                          i=pO[64:65, obase:obase + 512]: e.reciprocal(o, i), vs)
                    rec0 = P2.do("vector", lambda e, o=linv[0:1, 512:1024],
                                 i=pO[64:65, obase + 512:obase + 1024]:
                                 e.reciprocal(o, i), vs)
                    P2.w("vector", vs, rec0)
                    rec = P2.do("vector", lambda e, o=pv_scr[0:1, 0:4],
                                i=linv[0:1, 1020:1024]: e.tensor_copy(o, i), vs)
                    sl2 = rr[0] % 2
                    rr[0] += 1
                    gs, gv = pS_guard[sl2]
                    P2.w("tensor", gs, gv)
                    P2.w("tensor", vs, rec)
                    P2.do("tensor", lambda e,
                          o=pS[0:64, 1024 * sl2:1024 * sl2 + 512],
                          l=ones64[0:1, :], r=linv[0:1, 0:512]:
                          e.matmul(o, l, r, start=True, stop=True,
                                   skip_group_check=True), ts)
                    r1 = P2.do("tensor", lambda e,
                               o=pS[64:128, 1024 * sl2 + 512:1024 * sl2 + 1024],
                               l=ones64[0:1, :], r=linv[0:1, 512:1024]:
                               e.matmul(o, l, r, start=True, stop=True,
                                        tile_position=(0, 64),
                                        skip_group_check=True), ts)
                    P2.w("vector", ts, r1)
                    P2.do("vector", lambda e, o=yt[0:64, 512 * p:512 * p + 512],
                          b=pS[0:64, 1024 * sl2:1024 * sl2 + 512]:
                          e.tensor_mul(o, o, b), vs)
                    yev = P2.do("vector", lambda e,
                                o=yt[64:128, 512 * p:512 * p + 512],
                                b=pS[64:128, 1024 * sl2 + 512:1024 * sl2 + 1024]:
                                e.tensor_mul(o, o, b), vs)
                    pS_guard[sl2] = (vs, yev)
                    pO_vguard[p % 2] = yev
                qk_last[g] = qk_last_g

                if g == 0:
                    P2.w("tensor", d_wm["wo"], d_wm["wo"].n)
                P2.w("tensor", vs, pO_vguard[0])
                P2.w("tensor", vs, pO_vguard[1])
                addc = {}
                for tqc in range(4):
                    off = 1024 * (tqc % 2)
                    if tqc >= 2:
                        P2.w("tensor", vs, addc[tqc - 2])
                    op = 0
                    for hp in range(8):
                        for h in range(2):
                            fn = lambda e, o=pO[:, off + 512 * h:off + 512 * h + 512], \
                                        l=yt[:, 512 * hp + 128 * tqc:
                                             512 * hp + 128 * tqc + 128], \
                                        r=wos[:, 1024 * hp + 512 * h:
                                              1024 * hp + 512 * h + 512], \
                                        st=(hp == 0), sp=(hp == 7): \
                                e.matmul(o, l, r, start=st, stop=sp,
                                         skip_group_check=True)
                            op = P2.do("tensor", fn,
                                       ts if (hp == 7 and h == 1) else None)
                    for h in range(2):
                        i = pidx[0]
                        pidx[0] += 1
                        sl = i % 3
                        P2.w("scalar", g_out[sl], out_uses[sl])
                        row = TT * g + 128 * tqc
                        P2.do("scalar", lambda e, d=pbuf[sl][:],
                              s=part_scr[row:row + 128, 512 * h:512 * h + 512]:
                              e.dma_start(d, s), d_pt[sl], 16)
                        pt_val = d_pt[sl].n
                        P2.w("vector", ts, op)
                        P2.w("vector", d_pt[sl], pt_val)
                        ad = P2.do("vector", lambda e, o=pbuf[sl][:],
                                   a=pO[:, off + 512 * h:off + 512 * h + 512],
                                   b=pbuf[sl][:]: e.tensor_add(o, a, b), vs)
                        addc[tqc] = ad
                        P2.w("scalar", vs, ad)
                        P2.do("scalar", lambda e,
                              d=out_h[row:row + 128, 512 * h:512 * h + 512],
                              s=pbuf[sl][:]: e.dma_start(d, s), g_out[sl], 16)
                        out_uses[sl] = g_out[sl].n
                addc_prev = addc[3]

            for sl in range(3):
                P2.w("scalar", g_out[sl], g_out[sl].n)

            P2.emit(nc)

    return nc


_NC_CACHE = None


def kernel(L0_attn_x_q, L0_attn_w_q, L0_attn_w_k, L0_attn_w_v, L0_attn_w_o,
           L0_ffn_W_r, L0_ffn_W_k, L0_ffn_W_v, L0_ffn_mu_r, L0_ffn_mu_k):
    global _NC_CACHE
    x = np.ascontiguousarray(np.asarray(L0_attn_x_q, dtype=np.float32))
    bf = ml_dtypes.bfloat16

    def tr(w):
        return np.ascontiguousarray(
            np.asarray(w, dtype=np.float32).reshape(CDIM, CDIM).T)

    common = dict(
        wqT=tr(L0_attn_w_q), wkaT=tr(L0_attn_w_k), wvaT=tr(L0_attn_w_v),
        wo=np.ascontiguousarray(np.asarray(L0_attn_w_o, dtype=np.float32)),
        wrT=tr(L0_ffn_W_r), wkfT=tr(L0_ffn_W_k), wvfT=tr(L0_ffn_W_v),
        mur=np.ascontiguousarray(
            np.asarray(L0_ffn_mu_r, dtype=np.float32).reshape(8, 128).T),
        muk=np.ascontiguousarray(
            np.asarray(L0_ffn_mu_k, dtype=np.float32).reshape(8, 128).T),
        ident=np.eye(128, dtype=bf),
        maskd=np.concatenate(
            [np.where(np.arange(512)[None, :] < (np.arange(128)[:, None] + 128 * d),
                      np.float32(-30000.0), np.float32(0.0))
             for d in range(4)], axis=1).astype(bf),
        ones64=np.ones((1, 64), dtype=np.float32),
    )
    if _NC_CACHE is None:
        _NC_CACHE = build_nc()
    nc = _NC_CACHE
    in_maps = [dict(common, x=np.ascontiguousarray(x[i])) for i in range(8)]
    res = run_bass_kernel_spmd(nc, in_maps, core_ids=list(range(8)))
    return np.stack([r["out"] for r in res.results], axis=0)


if __name__ == "__main__":
    import reference as refmod
    inp = {k: np.asarray(v) for k, v in refmod.setup_inputs().items()}
    got = kernel(**inp)
    want = np.asarray(refmod.reference(**inp))
    denom = float(np.square(want).mean())
    print("resvar", float(np.square(got - want).mean() / denom),
          "maxabs", float(np.abs(got - want).max()))


# version 15 (final)
# speedup vs baseline: 1.0280x; 1.0112x over previous
import numpy as np
import ml_dtypes
import concourse.bass as bass
import concourse.mybir as mybir
from concourse.bass_utils import run_bass_kernel_spmd
from contextlib import ExitStack

F32 = mybir.dt.float32
BF16 = mybir.dt.bfloat16
AF = mybir.ActivationFunctionType
ALU = mybir.AluOpType
AX = mybir.AxisListType

T = 4096
CDIM = 1024
TT = 512
NT = T // TT
NB = T // 128
VW = 65
EPS = 1e-5

ENGINES = ("sync", "tensor", "scalar", "vector", "gpsimd")


class Sem:
    def __init__(self, h):
        self.h = h
        self.n = 0


class Plan:
    def __init__(self):
        self.ops = {e: [] for e in ENGINES}

    def w(self, e, sem, val):
        if val > 0:
            self.ops[e].append(lambda eng, s=sem.h, v=val: eng.wait_ge(s, v))

    def do(self, e, fn, sem=None, k=1):
        if sem is None:
            self.ops[e].append(fn)
            return None
        sem.n += k
        v = sem.n
        self.ops[e].append(
            lambda eng, f=fn, s=sem.h, kk=k: f(eng).then_inc(s, kk))
        return v

    def emit(self, nc):
        with nc.Block() as block:
            @block.sync
            def _(eng):
                for f in self.ops["sync"]:
                    f(eng)

            @block.tensor
            def _(eng):
                for f in self.ops["tensor"]:
                    f(eng)

            @block.scalar
            def _(eng):
                for f in self.ops["scalar"]:
                    f(eng)

            @block.vector
            def _(eng):
                for f in self.ops["vector"]:
                    f(eng)

            @block.gpsimd
            def _(eng):
                for f in self.ops["gpsimd"]:
                    f(eng)


def build_nc(debug=False):
    nc = bass.Bass("TRN2", target_bir_lowering=False, dynamic_dma_scratch_size=2048)

    x_h = nc.dram_tensor("x", [T, CDIM], F32, kind="ExternalInput").ap()
    wsrc = {}
    for m in ("wqT", "wkaT", "wvaT", "wo", "wrT", "wkfT", "wvfT"):
        wsrc[m] = nc.dram_tensor(m, [CDIM, CDIM], F32, kind="ExternalInput").ap()
    mur_h = nc.dram_tensor("mur", [128, 8], F32, kind="ExternalInput").ap()
    muk_h = nc.dram_tensor("muk", [128, 8], F32, kind="ExternalInput").ap()
    ident_h = nc.dram_tensor("ident", [128, 128], BF16, kind="ExternalInput").ap()
    maskd_h = nc.dram_tensor("maskd", [128, 2048], BF16, kind="ExternalInput").ap()
    ones_h = nc.dram_tensor("ones64", [1, 64], F32, kind="ExternalInput").ap()

    out_h = nc.dram_tensor("out", [T, CDIM], F32, kind="ExternalOutput").ap()

    scr_kind = "ExternalOutput" if debug else "Internal"
    xnT_scr = nc.dram_tensor("xnT_scr", [CDIM, T], BF16, kind=scr_kind).ap()
    qT_scr = nc.dram_tensor("qT_scr", [CDIM, T], BF16, kind=scr_kind).ap()
    part_scr = nc.dram_tensor("part_scr", [T, CDIM], F32, kind=scr_kind).ap()

    with ExitStack() as top:
        sbt = lambda n, sh, dt: top.enter_context(nc.sbuf_tensor(n, sh, dt))
        sem = lambda n: Sem(top.enter_context(nc.semaphore(n)))

        d_const = sem("d_const")
        d_wm = {m: sem(f"d_w_{m}") for m in wsrc}
        d_x = [sem("d_x0"), sem("d_x1")]
        d_xn2 = sem("d_xn2")
        d_qt = sem("d_qt")
        d_pt = [sem(f"d_pt{i}") for i in range(3)]
        g_q = [sem("g_q0"), sem("g_q1")]
        g_xn = sem("g_xn")
        g_po = [sem("g_po0"), sem("g_po1")]
        g_out = [sem(f"g_out{i}") for i in range(3)]
        ts = sem("ts")
        ss = sem("ss")
        vs = sem("vs")

        ident = sbt("ident_s", [128, 128], BF16)
        maskd = sbt("maskd_s", [128, 2048], BF16)
        ones64 = sbt("ones64_s", [1, 64], F32)
        mur = sbt("mur_s", [128, 8], F32)
        muk = sbt("muk_s", [128, 8], F32)
        epsb = sbt("epsb", [128, 1], F32)
        pv_scr = sbt("pv_scr", [128, 4], F32)
        ps_scr = sbt("ps_scr", [128, 8], F32)

        P1 = Plan()
        P1.do("sync", lambda e, d=ident[:], s=ident_h: e.dma_start(d, s), d_const, 16)
        P1.do("sync", lambda e, d=maskd[:], s=maskd_h: e.dma_start(d, s), d_const, 16)
        P1.do("sync", lambda e, d=ones64[:], s=ones_h: e.dma_start(d, s), d_const, 16)
        P1.do("sync", lambda e, d=mur[:], s=mur_h: e.dma_start(d, s), d_const, 16)
        P1.do("sync", lambda e, d=muk[:], s=muk_h: e.dma_start(d, s), d_const, 16)

        with ExitStack() as ph1:
            sb1 = lambda n, sh, dt: ph1.enter_context(nc.sbuf_tensor(n, sh, dt))
            wrTs = sb1("wrTs", [128, 8192], BF16)
            wkfTs = sb1("wkfTs", [128, 8192], BF16)
            wvfTs = sb1("wvfTs", [128, 8192], BF16)
            wqTs = sb1("wqTs", [128, 8192], BF16)
            xbuf = [sb1(f"xbuf{i}", [128, 4096], F32) for i in range(2)]
            xn_tok = sb1("xn_tok", [128, 4096], BF16)
            xnT = sb1("xnT", [128, 4096], BF16)
            dbufm = sb1("dbufm", [128, 512], F32)
            xrT = sb1("xrT", [128, 4096], BF16)
            xkT = sb1("xkT", [128, 4096], BF16)
            kactT = sb1("kactT", [128, 4096], BF16)
            rstore = sb1("rstore", [128, 4096], F32)
            scr4 = sb1("scr4", [128, 1024], F32)
            sqd = sb1("sqd", [128, 1024], F32)
            qe = [sb1(f"qe{i}", [128, 512], BF16) for i in range(2)]
            carry = sb1("carry", [128, 8], BF16)
            stat = sb1("stat", [128, 40], F32)
            stg = [sb1(f"stg{i}", [128, 1024], F32) for i in range(2)]

            SX, SXX, MEAN, EXX = slice(0, 4), slice(4, 8), slice(8, 12), slice(12, 16)
            M2, VAR, LNV, RSTD, NMR = (slice(16, 20), slice(20, 24), slice(24, 28),
                                       slice(28, 32), slice(32, 36))

            pA = ph1.enter_context(nc.psum_tensor("pA", [128, 2048], F32))
            pB = ph1.enter_context(nc.psum_tensor("pB", [128, 1024], F32))
            pC = ph1.enter_context(nc.psum_tensor("pC", [128, 2048], BF16))

            wdst1 = {"wrT": wrTs, "wkfT": wkfTs, "wvfT": wvfTs, "wqT": wqTs}
            stg_uses = [0, 0]
            wready = {}

            for name in ("wrT", "wkfT", "wvfT", "wqT"):
                for cc in range(8):
                    sl = cc % 2
                    P1.w("sync", vs, stg_uses[sl])
                    P1.do("sync", lambda e, d=stg[sl][:],
                          s=wsrc[name][128 * cc:128 * cc + 128, :]:
                          e.dma_start(d, s), d_wm[name], 16)
                    P1.w("vector", d_wm[name], 16 * (cc + 1))
                    stg_uses[sl] = P1.do(
                        "vector",
                        lambda e, d=wdst1[name][:, 1024 * cc:1024 * cc + 1024],
                        s=stg[sl][:]: e.tensor_copy(d, s), vs)
                wready[name] = stg_uses[(8 - 1) % 2]
            P1.do("vector", lambda e: e.memset(epsb[:], EPS), vs)
            P1.do("vector", lambda e: e.memset(carry[:], 0.0), vs)

            mix_done = {}
            evac_guard = {0: 0, 1: 0}
            pA_guard = {0: (ss, 0), 1: (ss, 0)}
            pB_guard = {0: (vs, 0), 1: (vs, 0)}
            last_ffn_vs = 0
            trans_done = {}
            qproj_done = {}
            g_q_uses = [0, 0]
            rsig_done = 0
            ke = 0

            for t in range(NT):
                xb = xbuf[t % 2]
                if t >= 2:
                    P1.w("sync", g_po[t % 2], 16 * (t // 2))
                P1.do("sync", lambda e, d=xb[:].rearrange("p (c f) -> p c f", c=4),
                      s=x_h[TT * t:TT * t + TT, :].rearrange("(c p) f -> p c f", p=128):
                      e.dma_start(d, s), d_x[t % 2], 16)
                dx_val = d_x[t % 2].n

                P1.w("scalar", d_x[t % 2], dx_val)
                for tc in range(4):
                    P1.do("scalar", lambda e, o=sqd[:],
                          i=xb[:, 1024 * tc:1024 * tc + 1024],
                          a=stat[:, SXX][:, tc:tc + 1]:
                          e.activation(o, i, AF.Square, accum_out=a), ss)
                sx_done = 0
                for tc in range(4):
                    sx_done = P1.do("scalar", lambda e, o=sqd[:],
                                    i=xb[:, 1024 * tc:1024 * tc + 1024],
                                    a=stat[:, SX][:, tc:tc + 1]:
                                    e.activation(o, i, AF.Identity, accum_out=a), ss)
                P1.w("scalar", ss, sx_done)
                statp = P1.do("scalar", lambda e, o=ps_scr[:], i=stat[:, 0:8]:
                              e.activation(o, i, AF.Identity), ss)
                P1.w("vector", ss, statp)
                mean_v = P1.do("vector", lambda e, o=stat[:, MEAN], i=stat[:, SX]:
                               e.tensor_scalar(o, i, 1.0 / CDIM, None, ALU.mult), vs)
                P1.do("vector", lambda e, o=stat[:, EXX], i=stat[:, SXX]:
                      e.tensor_scalar(o, i, 1.0 / CDIM, None, ALU.mult), vs)
                P1.w("vector", vs, mean_v)
                m2_v = P1.do("vector", lambda e, o=stat[:, M2], a=stat[:, MEAN]:
                             e.tensor_mul(o, a, a), vs)
                P1.w("vector", vs, m2_v)
                var_done = P1.do("vector", lambda e, o=stat[:, VAR], i=stat[:, M2],
                                 b=stat[:, EXX]:
                                 e.scalar_tensor_tensor(o, i, -1.0, b, ALU.mult,
                                                        ALU.add), vs)
                P1.w("vector", vs, var_done)
                varp = P1.do("vector", lambda e, o=pv_scr[:], i=stat[:, VAR]:
                             e.tensor_copy(o, i), vs)
                P1.w("scalar", vs, varp)
                P1.w("scalar", d_const, 80)
                lnv_v = P1.do("scalar", lambda e, o=stat[:, LNV], i=stat[:, VAR],
                              b=epsb[:]: e.activation(o, i, AF.Ln, bias=b), ss)
                P1.w("scalar", ss, lnv_v)
                rstd_done = P1.do("scalar", lambda e, o=stat[:, RSTD], i=stat[:, LNV]:
                                  e.activation(o, i, AF.Exp, scale=-0.5), ss)
                P1.w("scalar", ss, rstd_done)
                rstdp = P1.do("scalar", lambda e, o=ps_scr[:, 0:4], i=stat[:, RSTD]:
                              e.activation(o, i, AF.Identity), ss)
                P1.w("vector", ss, rstdp)
                nmr_done = P1.do("vector", lambda e, o=stat[:, NMR], i=stat[:, MEAN],
                                 r=stat[:, RSTD]:
                                 e.scalar_tensor_tensor(o, i, -1.0, r, ALU.mult,
                                                        ALU.mult), vs)
                P1.w("vector", vs, nmr_done)
                nmrp = P1.do("vector", lambda e, o=pv_scr[:], i=stat[:, NMR]:
                             e.tensor_copy(o, i), vs)
                P1.w("scalar", vs, nmrp)
                if t >= 1:
                    P1.w("scalar", ts, trans_done[t - 1])
                xn_done = 0
                for tc in range(4):
                    xn_done = P1.do("scalar", lambda e,
                                    o=xn_tok[:, 1024 * tc:1024 * tc + 1024],
                                    i=xb[:, 1024 * tc:1024 * tc + 1024],
                                    b=stat[:, NMR][:, tc:tc + 1],
                                    s=stat[:, RSTD][:, tc:tc + 1]:
                                    e.activation(o, i, AF.Identity, bias=b, scale=s),
                                    ss)

                xn_done = P1.do("scalar", lambda e, o=ps_scr[:, 4:8],
                                i=xn_tok[:, 4092:4096]:
                                e.activation(o, i, AF.Identity), ss)

                P1.w("tensor", ss, xn_done)
                if t == 0:
                    P1.w("tensor", d_const, 80)
                    P1.w("vector", d_const, 80)
                for cc in range(8):
                    half = cc % 2
                    P1.w("tensor", vs, evac_guard[half])
                    tdone = 0
                    for tc in range(4):
                        fn = lambda e, o=pC[:, 1024 * half + 128 * tc:
                                           1024 * half + 128 * tc + 128], \
                                    i=xn_tok[:, 1024 * tc + 128 * cc:
                                             1024 * tc + 128 * cc + 128], \
                                    idn=ident[:]: e.transpose(o, i, idn)
                        tdone = P1.do("tensor", fn, ts if tc == 3 else None)
                    if cc == 7:
                        trans_done[t] = tdone
                    P1.w("vector", ts, tdone)
                    if t >= 1 and cc == 0:
                        P1.w("vector", g_xn, 16 * t)
                        P1.w("vector", ts, qproj_done[t - 1])
                    ev = P1.do("vector", lambda e, o=xnT[:, 512 * cc:512 * cc + 512],
                               i=pC[:, 1024 * half:1024 * half + 512]:
                               e.tensor_copy(o, i), vs)
                    evac_guard[half] = ev
                    xsl = xnT[:, 512 * cc:512 * cc + 512]
                    P1.do("vector", lambda e, o=dbufm[:, 1:512], a=xsl[:, 1:512],
                          b=xsl[:, 0:511]: e.tensor_sub(o, a, b), vs)
                    P1.do("vector", lambda e, o=dbufm[:, 0:1], a=xsl[:, 0:1],
                          b=carry[:, cc:cc + 1]: e.tensor_sub(o, a, b), vs)
                    P1.do("vector", lambda e, o=xrT[:, 512 * cc + 1:512 * cc + 512],
                          i=dbufm[:, 1:512], m=mur[:, cc:cc + 1], b=xsl[:, 0:511]:
                          e.scalar_tensor_tensor(o, i, m, b, ALU.mult, ALU.add), vs)
                    P1.do("vector", lambda e, o=xrT[:, 512 * cc:512 * cc + 1],
                          i=dbufm[:, 0:1], m=mur[:, cc:cc + 1], b=carry[:, cc:cc + 1]:
                          e.scalar_tensor_tensor(o, i, m, b, ALU.mult, ALU.add), vs)
                    P1.do("vector", lambda e, o=xkT[:, 512 * cc + 1:512 * cc + 512],
                          i=dbufm[:, 1:512], m=muk[:, cc:cc + 1], b=xsl[:, 0:511]:
                          e.scalar_tensor_tensor(o, i, m, b, ALU.mult, ALU.add), vs)
                    md = P1.do("vector", lambda e, o=xkT[:, 512 * cc:512 * cc + 1],
                               i=dbufm[:, 0:1], m=muk[:, cc:cc + 1],
                               b=carry[:, cc:cc + 1]:
                               e.scalar_tensor_tensor(o, i, m, b, ALU.mult, ALU.add),
                               vs)
                    mix_done[cc] = md
                mix_last = P1.do(
                    "vector", lambda e, o=carry[:],
                    i=xnT[:].rearrange("p (c u) -> p c u", u=512)[:, :, 511:512]:
                    e.tensor_copy(o, i), vs)
                P1.w("scalar", vs, mix_last)
                P1.do("scalar", lambda e,
                      d=xnT_scr.rearrange("(cc p) u -> p cc u", p=128)[:, :, TT * t:TT * t + TT],
                      s=xnT[:].rearrange("p (cc u) -> p cc u", u=512):
                      e.dma_start(d, s), g_xn, 16)

                if t == 0:
                    P1.w("tensor", vs, wready["wrT"])
                for tc in range(4):
                    slot = tc % 2
                    gs, gv = pA_guard[slot]
                    P1.w("tensor", gs, gv)
                    rp = 0
                    for cin in range(8):
                        if tc == 0:
                            P1.w("tensor", vs, mix_done[cin])
                        for h in range(2):
                            fn = lambda e, o=pA[:, 1024 * slot + 512 * h:
                                               1024 * slot + 512 * h + 512], \
                                        l=xrT[:, 512 * cin + 128 * tc:
                                              512 * cin + 128 * tc + 128], \
                                        r=wrTs[:, 1024 * cin + 512 * h:
                                               1024 * cin + 512 * h + 512], \
                                        st=(cin == 0), sp=(cin == 7): \
                                e.matmul(o, l, r, start=st, stop=sp)
                            rp = P1.do("tensor", fn,
                                       ts if (cin == 7 and h == 1) else None)
                    P1.w("scalar", ts, rp)
                    if tc == 0:
                        P1.w("scalar", vs, last_ffn_vs)
                    else:
                        P1.w("scalar", vs, rsig_done)
                    rexp = P1.do("scalar", lambda e, o=scr4[:],
                                 i=pA[:, 1024 * slot:1024 * slot + 1024]:
                                 e.activation(o, i, AF.Exp, scale=-1.0), ss)
                    pA_guard[slot] = (ss, rexp)
                    P1.w("vector", ss, rexp)
                    P1.do("vector", lambda e, o=scr4[:]:
                          e.tensor_scalar(o, o, 1.0, None, ALU.add), vs)
                    rsig_done = P1.do(
                        "vector", lambda e, o=rstore[:, 1024 * tc:1024 * tc + 1024],
                        i=scr4[:]: e.reciprocal(o, i), vs)

                if t == 0:
                    P1.w("tensor", vs, wready["wkfT"])
                for co in range(8):
                    slot = co % 2
                    gs, gv = pB_guard[slot]
                    P1.w("tensor", gs, gv)
                    kp = 0
                    for cin in range(8):
                        fn = lambda e, o=pB[:, 512 * slot:512 * slot + 512], \
                                    l=wkfTs[:, 1024 * cin + 128 * co:
                                            1024 * cin + 128 * co + 128], \
                                    r=xkT[:, 512 * cin:512 * cin + 512], \
                                    st=(cin == 0), sp=(cin == 7): \
                            e.matmul(o, l, r, start=st, stop=sp)
                        kp = P1.do("tensor", fn, ts if cin == 7 else None)
                    P1.w("vector", ts, kp)
                    P1.w("vector", ss, rexp)
                    P1.do("vector", lambda e, o=scr4[:, 0:512],
                          i=pB[:, 512 * slot:512 * slot + 512]:
                          e.tensor_scalar(o, i, 0.0, None, ALU.max), vs)
                    ke = P1.do("vector", lambda e,
                               o=kactT[:, 512 * co:512 * co + 512],
                               a=scr4[:, 0:512]: e.tensor_mul(o, a, a), vs)
                    pB_guard[slot] = (vs, ke)

                if t == 0:
                    P1.w("tensor", vs, wready["wqT"])
                qp_last = 0
                for co in range(8):
                    slot = co % 2
                    gs, gv = pB_guard[slot]
                    P1.w("tensor", gs, gv)
                    qp = 0
                    for cin in range(8):
                        fn = lambda e, o=pB[:, 512 * slot:512 * slot + 512], \
                                    l=wqTs[:, 1024 * cin + 128 * co:
                                           1024 * cin + 128 * co + 128], \
                                    r=xnT[:, 512 * cin:512 * cin + 512], \
                                    st=(cin == 0), sp=(cin == 7): \
                            e.matmul(o, l, r, start=st, stop=sp)
                        qp = P1.do("tensor", fn, ts if cin == 7 else None)
                    qp_last = qp
                    P1.w("vector", ts, qp)
                    P1.w("vector", g_q[slot], g_q_uses[slot])
                    qev = P1.do("vector", lambda e, o=qe[slot][:],
                                i=pB[:, 512 * slot:512 * slot + 512]:
                                e.tensor_copy(o, i), vs)
                    pB_guard[slot] = (vs, qev)
                    P1.w("scalar", vs, qev)
                    P1.do("scalar", lambda e,
                          d=qT_scr[128 * co:128 * co + 128, TT * t:TT * t + TT],
                          s=qe[slot][:]: e.dma_start(d, s), g_q[slot], 16)
                    g_q_uses[slot] = g_q[slot].n
                qproj_done[t] = qp_last

                if t == 0:
                    P1.w("tensor", vs, wready["wvfT"])
                for tc in range(4):
                    slot = tc % 2
                    gs, gv = pA_guard[slot]
                    P1.w("tensor", gs, gv)
                    if tc == 0:
                        P1.w("tensor", vs, ke)
                    vp = 0
                    for din in range(8):
                        for h in range(2):
                            fn = lambda e, o=pA[:, 1024 * slot + 512 * h:
                                               1024 * slot + 512 * h + 512], \
                                        l=kactT[:, 512 * din + 128 * tc:
                                                512 * din + 128 * tc + 128], \
                                        r=wvfTs[:, 1024 * din + 512 * h:
                                                1024 * din + 512 * h + 512], \
                                        st=(din == 0), sp=(din == 7): \
                                e.matmul(o, l, r, start=st, stop=sp)
                            vp = P1.do("tensor", fn,
                                       ts if (din == 7 and h == 1) else None)
                    P1.w("vector", ts, vp)
                    P1.w("vector", ss, xn_done)
                    P1.do("vector", lambda e, o=scr4[:],
                          a=pA[:, 1024 * slot:1024 * slot + 1024],
                          b=rstore[:, 1024 * tc:1024 * tc + 1024]:
                          e.tensor_mul(o, a, b), vs)
                    ffn = P1.do("vector",
                                lambda e, o=xb[:, 1024 * tc:1024 * tc + 1024],
                                a=scr4[:], b=xb[:, 1024 * tc:1024 * tc + 1024]:
                                e.tensor_add(o, a, b), vs)
                    pA_guard[slot] = (vs, ffn)
                    last_ffn_vs = ffn
                P1.w("scalar", vs, last_ffn_vs)
                P1.do("scalar", lambda e,
                      d=part_scr[TT * t:TT * t + TT, :].rearrange("(c p) f -> p c f", p=128),
                      s=xb[:].rearrange("p (c f) -> p c f", c=4):
                      e.dma_start(d, s), g_po[t % 2], 16)

            ph1_ts, ph1_ss, ph1_vs = ts.n, ss.n, vs.n
            P1.emit(nc)

        P2 = Plan()
        with ExitStack() as ph2:
            sb2 = lambda n, sh, dt: ph2.enter_context(nc.sbuf_tensor(n, sh, dt))
            wkaTs = sb2("wkaTs", [128, 8192], BF16)
            wvaTs = sb2("wvaTs", [128, 8192], BF16)
            wos = sb2("wos", [128, 8192], BF16)
            Kst = sb2("Kst", [128, 32768], BF16)
            Vst = sb2("Vst", [128, NB * 1040], BF16)
            xn2 = sb2("xn2", [128, 4096], BF16)
            qt = sb2("qt", [128, 4096], BF16)
            ptb = sb2("ptb", [128, 2048], BF16)
            yt = sb2("yt", [128, 4096], BF16)
            pbuf = [sb2(f"pbuf{i}", [128, 512], F32) for i in range(3)]
            stg2 = [sb2("stg2_0", [128, 1024], F32)]
            linv = stg2[0][0:1, :]

            pS = ph2.enter_context(nc.psum_tensor("pS", [128, 2048], F32))
            pO = ph2.enter_context(nc.psum_tensor("pO", [128, 2048], F32))

            for e in ENGINES:
                P2.w(e, ts, ph1_ts)
                P2.w(e, ss, ph1_ss)
                P2.w(e, vs, ph1_vs)
                for s2 in g_po + g_q + [g_xn]:
                    P2.w(e, s2, s2.n)

            wdst2 = {"wkaT": wkaTs, "wvaT": wvaTs, "wo": wos}
            stg2_uses = [0, 0]
            w2ready = 0
            for name in ("wkaT", "wvaT", "wo"):
                for cc in range(8):
                    P2.w("sync", vs, w2ready)
                    P2.do("sync", lambda e, d=stg2[0][:],
                          s=wsrc[name][128 * cc:128 * cc + 128, :]:
                          e.dma_start(d, s), d_wm[name], 16)
                    P2.w("vector", d_wm[name], d_wm[name].n)
                    w2ready = P2.do(
                        "vector",
                        lambda e, d=wdst2[name][:, 1024 * cc:1024 * cc + 1024],
                        s=stg2[0][:]: e.tensor_copy(d, s), vs)
            vones = P2.do("vector", lambda e,
                          o=Vst[:].rearrange("p (x c) -> p x c", c=VW)[:, :, 64:65]:
                          e.memset(o, 1.0), vs)

            pS_guard = [(ss, 0), (ss, 0)]
            pO_vguard = {0: 0, 1: 0}
            vproj_done = {}
            qk_last = {}
            addc_prev = 0
            rr = [0]
            out_uses = [0, 0, 0]
            pidx = [0]

            def load_xn2(g):
                P2.do("sync", lambda e, d=xn2[:].rearrange("p (cc u) -> p cc u", u=512),
                      s=xnT_scr.rearrange("(cc p) u -> p cc u", p=128)[:, :, TT * g:TT * g + TT]:
                      e.dma_start(d, s), d_xn2, 16)

            def load_qt(g):
                P2.do("sync", lambda e, d=qt[:].rearrange("p (cc u) -> p cc u", u=512),
                      s=qT_scr.rearrange("(cc p) u -> p cc u", p=128)[:, :, TT * g:TT * g + TT]:
                      e.dma_start(d, s), d_qt, 16)

            load_xn2(0)
            load_qt(0)
            for g in range(NT):
                P2.w("tensor", d_xn2, 16 * (g + 1))
                if g == 0:
                    P2.w("tensor", vs, w2ready)
                for pr in range(8):
                    half = pr % 2
                    gs, gv = pS_guard[half]
                    P2.w("tensor", gs, gv)
                    kp = 0
                    for cin in range(8):
                        fn = lambda e, o=pS[:, 1024 * half:1024 * half + 512], \
                                    l=wkaTs[:, 1024 * cin + 128 * pr:
                                            1024 * cin + 128 * pr + 128], \
                                    r=xn2[:, 512 * cin:512 * cin + 512], \
                                    st=(cin == 0), sp=(cin == 7): \
                            e.matmul(o, l, r, start=st, stop=sp)
                        kp = P2.do("tensor", fn, ts if cin == 7 else None)
                    P2.w("vector", ts, kp)
                    kev = P2.do("vector", lambda e,
                                o=Kst[:, 4096 * pr + TT * g:4096 * pr + TT * g + 512],
                                i=pS[:, 1024 * half:1024 * half + 512]:
                                e.tensor_copy(o, i), vs)
                    pS_guard[half] = (vs, kev)
                vev = 0
                if g >= 1:
                    P2.w("tensor", vs, addc_prev)
                for tkc in range(4):
                    half = tkc % 2
                    P2.w("tensor", vs, pO_vguard[half])
                    vp = 0
                    for cin in range(8):
                        for h in range(2):
                            fn = lambda e, o=pO[:, 1024 * half + 512 * h:
                                               1024 * half + 512 * h + 512], \
                                        l=xn2[:, 512 * cin + 128 * tkc:
                                              512 * cin + 128 * tkc + 128], \
                                        r=wvaTs[:, 1024 * cin + 512 * h:
                                                1024 * cin + 512 * h + 512], \
                                        st=(cin == 0), sp=(cin == 7): \
                                e.matmul(o, l, r, start=st, stop=sp)
                            vp = P2.do("tensor", fn,
                                       ts if (cin == 7 and h == 1) else None)
                    if tkc == 3:
                        vproj_done[g] = vp
                    P2.w("vector", ts, vp)
                    blk = 4 * g + tkc
                    vev = P2.do("vector", lambda e,
                                o=Vst[:, 1040 * blk:1040 * blk + 1040].rearrange(
                                    "p (h c) -> p h c", c=VW)[:, :, 0:64],
                                i=pO[:, 1024 * half:1024 * half + 1024].rearrange(
                                    "p (h c) -> p h c", c=64):
                                e.tensor_copy(o, i), vs)
                    pO_vguard[half] = vev
                kv_ev = vev
                if g + 1 < NT:
                    P2.w("sync", ts, vproj_done[g])
                    load_xn2(g + 1)

                J = 4 * g + 4
                P2.w("tensor", d_qt, 16 * (g + 1))
                P2.w("tensor", vs, kv_ev)
                qk_last_g = 0
                for p in range(8):
                    obase = 1024 * (p % 2)
                    expc = {}
                    qkc = {}
                    slot_of = {}

                    def emit_pv(j, Jv=J, pv=p, ob=obase, ec=expc,
                                so=slot_of):
                        P2.w("tensor", ss, ec[j])
                        if j == 0:
                            P2.w("tensor", vs, pO_vguard[pv % 2])
                        sl = so[j]
                        P2.do("tensor", lambda e,
                              o=pO[0:65, ob:ob + 512],
                              l=Vst[:, 1040 * j + VW * 2 * pv:
                                    1040 * j + VW * 2 * pv + VW],
                              r=ptb[:, 1024 * sl:1024 * sl + 512],
                              st=(j == 0), sp=(j == Jv - 1):
                              e.matmul(o, l, r, start=st, stop=sp,
                                       skip_group_check=True), None)
                        return P2.do("tensor", lambda e,
                                     o=pO[0:65, ob + 512:ob + 1024],
                                     l=Vst[:, 1040 * j + VW * (2 * pv + 1):
                                           1040 * j + VW * (2 * pv + 1) + VW],
                                     r=ptb[:, 1024 * sl + 512:1024 * sl + 1024],
                                     st=(j == 0), sp=(j == Jv - 1):
                                     e.matmul(o, l, r, start=st, stop=sp,
                                              skip_group_check=True),
                                     ts if j == Jv - 1 else None)

                    for j in range(J):
                        sl = rr[0] % 2
                        rr[0] += 1
                        slot_of[j] = sl
                        diag = j >= 4 * g
                        gs, gv = pS_guard[sl]
                        P2.w("tensor", gs, gv)
                        P2.do("tensor", lambda e,
                              o=pS[:, 1024 * sl:1024 * sl + 512],
                              l=Kst[0:64, 4096 * p + 128 * j:4096 * p + 128 * j + 128],
                              r=qt[0:64, 512 * p:512 * p + 512], sp=(not diag):
                              e.matmul(o, l, r, start=True, stop=sp,
                                       tile_position=(0, 0),
                                       skip_group_check=True), ts)
                        qkc[j] = P2.do("tensor", lambda e,
                                       o=pS[:, 1024 * sl + 512:1024 * sl + 1024],
                                       l=Kst[64:128, 4096 * p + 128 * j:
                                             4096 * p + 128 * j + 128],
                                       r=qt[64:128, 512 * p:512 * p + 512],
                                       sp=(not diag):
                                       e.matmul(o, l, r, start=True, stop=sp,
                                                tile_position=(64, 0),
                                                skip_group_check=True), ts)
                        if diag:
                            P2.do("tensor", lambda e,
                                  o=pS[:, 1024 * sl:1024 * sl + 512],
                                  l=ident[:],
                                  r=maskd[:, 512 * (j - 4 * g):
                                          512 * (j - 4 * g) + 512]:
                                  e.matmul(o, l, r, start=False, stop=True,
                                           skip_group_check=True), ts)
                            qkc[j] = P2.do("tensor", lambda e,
                                           o=pS[:, 1024 * sl + 512:1024 * sl + 1024],
                                           l=ident[:],
                                           r=maskd[:, 512 * (j - 4 * g):
                                                   512 * (j - 4 * g) + 512]:
                                           e.matmul(o, l, r, start=False, stop=True,
                                                    skip_group_check=True), ts)
                        P2.w("scalar", ts, qkc[j])
                        expc[j] = P2.do("scalar", lambda e,
                                        o=ptb[:, 1024 * sl:1024 * sl + 1024],
                                        i=pS[:, 1024 * sl:1024 * sl + 1024]:
                                        e.activation(o, i, AF.Exp, scale=0.125), ss)
                        pS_guard[sl] = (ss, expc[j])
                        if j >= 1:
                            emit_pv(j - 1)
                    pvlast = emit_pv(J - 1)
                    qk_last_g = qkc[J - 1]

                    P2.w("vector", ts, pvlast)
                    P2.do("vector", lambda e, o=yt[0:64, 512 * p:512 * p + 512],
                          i=pO[0:64, obase:obase + 512]: e.tensor_copy(o, i), vs)
                    P2.do("vector", lambda e, o=yt[64:128, 512 * p:512 * p + 512],
                          i=pO[0:64, obase + 512:obase + 1024]:
                          e.tensor_copy(o, i), vs)
                    P2.do("vector", lambda e, o=linv[0:1, 0:512],
                          i=pO[64:65, obase:obase + 512]: e.reciprocal(o, i), vs)
                    rec0 = P2.do("vector", lambda e, o=linv[0:1, 512:1024],
                                 i=pO[64:65, obase + 512:obase + 1024]:
                                 e.reciprocal(o, i), vs)
                    P2.w("vector", vs, rec0)
                    rec = P2.do("vector", lambda e, o=pv_scr[0:1, 0:4],
                                i=linv[0:1, 1020:1024]: e.tensor_copy(o, i), vs)
                    sl2 = rr[0] % 2
                    rr[0] += 1
                    gs, gv = pS_guard[sl2]
                    P2.w("tensor", gs, gv)
                    P2.w("tensor", vs, rec)
                    P2.do("tensor", lambda e,
                          o=pS[0:64, 1024 * sl2:1024 * sl2 + 512],
                          l=ones64[0:1, :], r=linv[0:1, 0:512]:
                          e.matmul(o, l, r, start=True, stop=True,
                                   skip_group_check=True), ts)
                    r1 = P2.do("tensor", lambda e,
                               o=pS[64:128, 1024 * sl2 + 512:1024 * sl2 + 1024],
                               l=ones64[0:1, :], r=linv[0:1, 512:1024]:
                               e.matmul(o, l, r, start=True, stop=True,
                                        tile_position=(0, 64),
                                        skip_group_check=True), ts)
                    P2.w("vector", ts, r1)
                    P2.do("vector", lambda e, o=yt[0:64, 512 * p:512 * p + 512],
                          b=pS[0:64, 1024 * sl2:1024 * sl2 + 512]:
                          e.tensor_mul(o, o, b), vs)
                    yev = P2.do("vector", lambda e,
                                o=yt[64:128, 512 * p:512 * p + 512],
                                b=pS[64:128, 1024 * sl2 + 512:1024 * sl2 + 1024]:
                                e.tensor_mul(o, o, b), vs)
                    pS_guard[sl2] = (vs, yev)
                    pO_vguard[p % 2] = yev
                qk_last[g] = qk_last_g
                if g + 1 < NT:
                    P2.w("sync", ts, qk_last[g])
                    load_qt(g + 1)

                if g == 0:
                    P2.w("tensor", d_wm["wo"], d_wm["wo"].n)
                P2.w("tensor", vs, pO_vguard[0])
                P2.w("tensor", vs, pO_vguard[1])
                addc = {}
                for tqc in range(4):
                    off = 1024 * (tqc % 2)
                    if tqc >= 2:
                        P2.w("tensor", vs, addc[tqc - 2])
                    op = 0
                    for hp in range(8):
                        for h in range(2):
                            fn = lambda e, o=pO[:, off + 512 * h:off + 512 * h + 512], \
                                        l=yt[:, 512 * hp + 128 * tqc:
                                             512 * hp + 128 * tqc + 128], \
                                        r=wos[:, 1024 * hp + 512 * h:
                                              1024 * hp + 512 * h + 512], \
                                        st=(hp == 0), sp=(hp == 7): \
                                e.matmul(o, l, r, start=st, stop=sp,
                                         skip_group_check=True)
                            op = P2.do("tensor", fn,
                                       ts if (hp == 7 and h == 1) else None)
                    for h in range(2):
                        i = pidx[0]
                        pidx[0] += 1
                        sl = i % 3
                        P2.w("sync", g_out[sl], out_uses[sl])
                        row = TT * g + 128 * tqc
                        P2.do("sync", lambda e, d=pbuf[sl][:],
                              s=part_scr[row:row + 128, 512 * h:512 * h + 512]:
                              e.dma_start(d, s), d_pt[sl], 16)
                        pt_val = d_pt[sl].n
                        P2.w("vector", ts, op)
                        P2.w("vector", d_pt[sl], pt_val)
                        ad = P2.do("vector", lambda e, o=pbuf[sl][:],
                                   a=pO[:, off + 512 * h:off + 512 * h + 512],
                                   b=pbuf[sl][:]: e.tensor_add(o, a, b), vs)
                        addc[tqc] = ad
                        P2.w("sync", vs, ad)
                        P2.do("sync", lambda e,
                              d=out_h[row:row + 128, 512 * h:512 * h + 512],
                              s=pbuf[sl][:]: e.dma_start(d, s), g_out[sl], 16)
                        out_uses[sl] = g_out[sl].n
                addc_prev = addc[3]

            for sl in range(3):
                P2.w("sync", g_out[sl], g_out[sl].n)

            P2.emit(nc)

    return nc


_NC_CACHE = None


def kernel(L0_attn_x_q, L0_attn_w_q, L0_attn_w_k, L0_attn_w_v, L0_attn_w_o,
           L0_ffn_W_r, L0_ffn_W_k, L0_ffn_W_v, L0_ffn_mu_r, L0_ffn_mu_k):
    global _NC_CACHE
    x = np.ascontiguousarray(np.asarray(L0_attn_x_q, dtype=np.float32))
    bf = ml_dtypes.bfloat16

    def tr(w):
        return np.ascontiguousarray(
            np.asarray(w, dtype=np.float32).reshape(CDIM, CDIM).T)

    common = dict(
        wqT=tr(L0_attn_w_q), wkaT=tr(L0_attn_w_k), wvaT=tr(L0_attn_w_v),
        wo=np.ascontiguousarray(np.asarray(L0_attn_w_o, dtype=np.float32)),
        wrT=tr(L0_ffn_W_r), wkfT=tr(L0_ffn_W_k), wvfT=tr(L0_ffn_W_v),
        mur=np.ascontiguousarray(
            np.asarray(L0_ffn_mu_r, dtype=np.float32).reshape(8, 128).T),
        muk=np.ascontiguousarray(
            np.asarray(L0_ffn_mu_k, dtype=np.float32).reshape(8, 128).T),
        ident=np.eye(128, dtype=bf),
        maskd=np.concatenate(
            [np.where(np.arange(512)[None, :] < (np.arange(128)[:, None] + 128 * d),
                      np.float32(-30000.0), np.float32(0.0))
             for d in range(4)], axis=1).astype(bf),
        ones64=np.ones((1, 64), dtype=np.float32),
    )
    if _NC_CACHE is None:
        _NC_CACHE = build_nc()
    nc = _NC_CACHE
    in_maps = [dict(common, x=np.ascontiguousarray(x[i])) for i in range(8)]
    res = run_bass_kernel_spmd(nc, in_maps, core_ids=list(range(8)))
    return np.stack([r["out"] for r in res.results], axis=0)


if __name__ == "__main__":
    import reference as refmod
    inp = {k: np.asarray(v) for k, v in refmod.setup_inputs().items()}
    got = kernel(**inp)
    want = np.asarray(refmod.reference(**inp))
    denom = float(np.square(want).mean())
    print("resvar", float(np.square(got - want).mean() / denom),
          "maxabs", float(np.abs(got - want).max()))


# version 16
# speedup vs baseline: 1.0497x; 1.0211x over previous
import numpy as np
import ml_dtypes
import concourse.bass as bass
import concourse.mybir as mybir
from concourse.bass_utils import run_bass_kernel_spmd
from contextlib import ExitStack

F32 = mybir.dt.float32
BF16 = mybir.dt.bfloat16
AF = mybir.ActivationFunctionType
ALU = mybir.AluOpType
AX = mybir.AxisListType

T = 4096
CDIM = 1024
TT = 512
NT = T // TT
NB = T // 128
VW = 65
EPS = 1e-5

ENGINES = ("sync", "tensor", "scalar", "vector", "gpsimd")


class Sem:
    def __init__(self, h):
        self.h = h
        self.n = 0


class Plan:
    def __init__(self):
        self.ops = {e: [] for e in ENGINES}

    def w(self, e, sem, val):
        if val > 0:
            self.ops[e].append(lambda eng, s=sem.h, v=val: eng.wait_ge(s, v))

    def do(self, e, fn, sem=None, k=1):
        if sem is None:
            self.ops[e].append(fn)
            return None
        sem.n += k
        v = sem.n
        self.ops[e].append(
            lambda eng, f=fn, s=sem.h, kk=k: f(eng).then_inc(s, kk))
        return v

    def emit(self, nc):
        with nc.Block() as block:
            @block.sync
            def _(eng):
                for f in self.ops["sync"]:
                    f(eng)

            @block.tensor
            def _(eng):
                for f in self.ops["tensor"]:
                    f(eng)

            @block.scalar
            def _(eng):
                for f in self.ops["scalar"]:
                    f(eng)

            @block.vector
            def _(eng):
                for f in self.ops["vector"]:
                    f(eng)

            @block.gpsimd
            def _(eng):
                for f in self.ops["gpsimd"]:
                    f(eng)


def build_nc(debug=False):
    nc = bass.Bass("TRN2", target_bir_lowering=False, dynamic_dma_scratch_size=2048)

    x_h = nc.dram_tensor("x", [T, CDIM], F32, kind="ExternalInput").ap()
    wsrc = {}
    for m in ("wqT", "wkaT", "wvaT", "wo", "wrT", "wkfT", "wvfT"):
        wsrc[m] = nc.dram_tensor(m, [CDIM, CDIM], F32, kind="ExternalInput").ap()
    mur_h = nc.dram_tensor("mur", [128, 8], F32, kind="ExternalInput").ap()
    muk_h = nc.dram_tensor("muk", [128, 8], F32, kind="ExternalInput").ap()
    ident_h = nc.dram_tensor("ident", [128, 128], BF16, kind="ExternalInput").ap()
    maskd_h = nc.dram_tensor("maskd", [128, 2048], BF16, kind="ExternalInput").ap()
    ones_h = nc.dram_tensor("ones64", [1, 64], F32, kind="ExternalInput").ap()

    out_h = nc.dram_tensor("out", [T, CDIM], F32, kind="ExternalOutput").ap()

    scr_kind = "ExternalOutput" if debug else "Internal"
    xnT_scr = nc.dram_tensor("xnT_scr", [CDIM, T], BF16, kind=scr_kind).ap()
    qT_scr = nc.dram_tensor("qT_scr", [CDIM, T], BF16, kind=scr_kind).ap()
    part_scr = nc.dram_tensor("part_scr", [T, CDIM], F32, kind=scr_kind).ap()

    with ExitStack() as top:
        sbt = lambda n, sh, dt: top.enter_context(nc.sbuf_tensor(n, sh, dt))
        sem = lambda n: Sem(top.enter_context(nc.semaphore(n)))

        d_const = sem("d_const")
        d_wm = {m: sem(f"d_w_{m}") for m in wsrc}
        d_x = [sem("d_x0"), sem("d_x1")]
        d_xn2 = sem("d_xn2")
        d_qt = sem("d_qt")
        d_pt = [sem(f"d_pt{i}") for i in range(3)]
        g_q = [sem("g_q0"), sem("g_q1")]
        g_xn = sem("g_xn")
        g_po = [sem("g_po0"), sem("g_po1")]
        g_out = [sem(f"g_out{i}") for i in range(3)]
        ts = sem("ts")
        ss = sem("ss")
        vs = sem("vs")

        ident = sbt("ident_s", [128, 128], BF16)
        maskd = sbt("maskd_s", [128, 2048], BF16)
        ones64 = sbt("ones64_s", [1, 64], F32)
        mur = sbt("mur_s", [128, 8], F32)
        muk = sbt("muk_s", [128, 8], F32)
        epsb = sbt("epsb", [128, 1], F32)
        pv_scr = sbt("pv_scr", [128, 4], F32)
        ps_scr = sbt("ps_scr", [128, 8], F32)

        P1 = Plan()
        P1.do("sync", lambda e, d=ident[:], s=ident_h: e.dma_start(d, s), d_const, 16)
        P1.do("sync", lambda e, d=maskd[:], s=maskd_h: e.dma_start(d, s), d_const, 16)
        P1.do("sync", lambda e, d=ones64[:], s=ones_h: e.dma_start(d, s), d_const, 16)
        P1.do("sync", lambda e, d=mur[:], s=mur_h: e.dma_start(d, s), d_const, 16)
        P1.do("sync", lambda e, d=muk[:], s=muk_h: e.dma_start(d, s), d_const, 16)

        with ExitStack() as ph1:
            sb1 = lambda n, sh, dt: ph1.enter_context(nc.sbuf_tensor(n, sh, dt))
            wrTs = sb1("wrTs", [128, 8192], BF16)
            wkfTs = sb1("wkfTs", [128, 8192], BF16)
            wvfTs = sb1("wvfTs", [128, 8192], BF16)
            wqTs = sb1("wqTs", [128, 8192], BF16)
            xbuf = [sb1(f"xbuf{i}", [128, 4096], F32) for i in range(2)]
            xn_tok = sb1("xn_tok", [128, 4096], BF16)
            xnT = sb1("xnT", [128, 4096], BF16)
            dbufm = sb1("dbufm", [128, 512], F32)
            xrT = sb1("xrT", [128, 4096], BF16)
            xkT = sb1("xkT", [128, 4096], BF16)
            kactT = sb1("kactT", [128, 4096], BF16)
            rstore = sb1("rstore", [128, 4096], F32)
            scr4 = sb1("scr4", [128, 1024], F32)
            sqd = sb1("sqd", [128, 1024], F32)
            qe = [sb1(f"qe{i}", [128, 512], BF16) for i in range(2)]
            carry = sb1("carry", [128, 8], BF16)
            stat = sb1("stat", [128, 40], F32)
            stg = [sb1(f"stg{i}", [128, 1024], F32) for i in range(2)]

            SX, SXX, MEAN, EXX = slice(0, 4), slice(4, 8), slice(8, 12), slice(12, 16)
            M2, VAR, LNV, RSTD, NMR = (slice(16, 20), slice(20, 24), slice(24, 28),
                                       slice(28, 32), slice(32, 36))

            pA = ph1.enter_context(nc.psum_tensor("pA", [128, 2048], F32))
            pB = ph1.enter_context(nc.psum_tensor("pB", [128, 1024], F32))
            pC = ph1.enter_context(nc.psum_tensor("pC", [128, 2048], BF16))

            wdst1 = {"wrT": wrTs, "wkfT": wkfTs, "wvfT": wvfTs, "wqT": wqTs}
            stg_uses = [0, 0]
            wready = {}

            for name in ("wrT", "wkfT", "wvfT", "wqT"):
                for cc in range(8):
                    sl = cc % 2
                    P1.w("sync", vs, stg_uses[sl])
                    P1.do("sync", lambda e, d=stg[sl][:],
                          s=wsrc[name][128 * cc:128 * cc + 128, :]:
                          e.dma_start(d, s), d_wm[name], 16)
                    P1.w("vector", d_wm[name], 16 * (cc + 1))
                    stg_uses[sl] = P1.do(
                        "vector",
                        lambda e, d=wdst1[name][:, 1024 * cc:1024 * cc + 1024],
                        s=stg[sl][:]: e.tensor_copy(d, s), vs)
                wready[name] = stg_uses[(8 - 1) % 2]
            P1.do("vector", lambda e: e.memset(epsb[:], EPS), vs)
            P1.do("vector", lambda e: e.memset(carry[:], 0.0), vs)

            mix_done = {}
            evac_guard = {0: 0, 1: 0}
            pA_guard = {0: (ss, 0), 1: (ss, 0)}
            pB_guard = {0: (vs, 0), 1: (vs, 0)}
            last_ffn_vs = 0
            trans_done = {}
            qproj_done = {}
            g_q_uses = [0, 0]
            rsig_done = 0
            ke = 0

            def load_x(t):
                if t >= 2:
                    P1.w("sync", g_po[t % 2], 16 * (t // 2))
                P1.do("sync", lambda e,
                      d=xbuf[t % 2][:].rearrange("p (c f) -> p c f", c=4),
                      s=x_h[TT * t:TT * t + TT, :].rearrange("(c p) f -> p c f", p=128):
                      e.dma_start(d, s), d_x[t % 2], 16)

            load_x(0)
            load_x(1)
            for t in range(NT):
                xb = xbuf[t % 2]
                dx_val = 16 * (t // 2 + 1)

                P1.w("scalar", d_x[t % 2], dx_val)
                for tc in range(4):
                    P1.do("scalar", lambda e, o=sqd[:],
                          i=xb[:, 1024 * tc:1024 * tc + 1024],
                          a=stat[:, SXX][:, tc:tc + 1]:
                          e.activation(o, i, AF.Square, accum_out=a), ss)
                sx_done = 0
                for tc in range(4):
                    sx_done = P1.do("scalar", lambda e, o=sqd[:],
                                    i=xb[:, 1024 * tc:1024 * tc + 1024],
                                    a=stat[:, SX][:, tc:tc + 1]:
                                    e.activation(o, i, AF.Identity, accum_out=a), ss)
                P1.w("scalar", ss, sx_done)
                statp = P1.do("scalar", lambda e, o=ps_scr[:], i=stat[:, 0:8]:
                              e.activation(o, i, AF.Identity), ss)
                P1.w("vector", ss, statp)
                mean_v = P1.do("vector", lambda e, o=stat[:, MEAN], i=stat[:, SX]:
                               e.tensor_scalar(o, i, 1.0 / CDIM, None, ALU.mult), vs)
                P1.do("vector", lambda e, o=stat[:, EXX], i=stat[:, SXX]:
                      e.tensor_scalar(o, i, 1.0 / CDIM, None, ALU.mult), vs)
                P1.w("vector", vs, mean_v)
                m2_v = P1.do("vector", lambda e, o=stat[:, M2], a=stat[:, MEAN]:
                             e.tensor_mul(o, a, a), vs)
                P1.w("vector", vs, m2_v)
                var_done = P1.do("vector", lambda e, o=stat[:, VAR], i=stat[:, M2],
                                 b=stat[:, EXX]:
                                 e.scalar_tensor_tensor(o, i, -1.0, b, ALU.mult,
                                                        ALU.add), vs)
                P1.w("vector", vs, var_done)
                varp = P1.do("vector", lambda e, o=pv_scr[:], i=stat[:, VAR]:
                             e.tensor_copy(o, i), vs)
                P1.w("scalar", vs, varp)
                P1.w("scalar", d_const, 80)
                lnv_v = P1.do("scalar", lambda e, o=stat[:, LNV], i=stat[:, VAR],
                              b=epsb[:]: e.activation(o, i, AF.Ln, bias=b), ss)
                P1.w("scalar", ss, lnv_v)
                rstd_done = P1.do("scalar", lambda e, o=stat[:, RSTD], i=stat[:, LNV]:
                                  e.activation(o, i, AF.Exp, scale=-0.5), ss)
                P1.w("scalar", ss, rstd_done)
                rstdp = P1.do("scalar", lambda e, o=ps_scr[:, 0:4], i=stat[:, RSTD]:
                              e.activation(o, i, AF.Identity), ss)
                P1.w("vector", ss, rstdp)
                nmr_done = P1.do("vector", lambda e, o=stat[:, NMR], i=stat[:, MEAN],
                                 r=stat[:, RSTD]:
                                 e.scalar_tensor_tensor(o, i, -1.0, r, ALU.mult,
                                                        ALU.mult), vs)
                P1.w("vector", vs, nmr_done)
                nmrp = P1.do("vector", lambda e, o=pv_scr[:], i=stat[:, NMR]:
                             e.tensor_copy(o, i), vs)
                P1.w("scalar", vs, nmrp)
                if t >= 1:
                    P1.w("scalar", ts, trans_done[t - 1])
                xn_done = 0
                for tc in range(4):
                    xn_done = P1.do("scalar", lambda e,
                                    o=xn_tok[:, 1024 * tc:1024 * tc + 1024],
                                    i=xb[:, 1024 * tc:1024 * tc + 1024],
                                    b=stat[:, NMR][:, tc:tc + 1],
                                    s=stat[:, RSTD][:, tc:tc + 1]:
                                    e.activation(o, i, AF.Identity, bias=b, scale=s),
                                    ss)

                xn_done = P1.do("scalar", lambda e, o=ps_scr[:, 4:8],
                                i=xn_tok[:, 4092:4096]:
                                e.activation(o, i, AF.Identity), ss)

                P1.w("tensor", ss, xn_done)
                if t == 0:
                    P1.w("tensor", d_const, 80)
                    P1.w("vector", d_const, 80)
                for cc in range(8):
                    half = cc % 2
                    P1.w("tensor", vs, evac_guard[half])
                    tdone = 0
                    for tc in range(4):
                        fn = lambda e, o=pC[:, 1024 * half + 128 * tc:
                                           1024 * half + 128 * tc + 128], \
                                    i=xn_tok[:, 1024 * tc + 128 * cc:
                                             1024 * tc + 128 * cc + 128], \
                                    idn=ident[:]: e.transpose(o, i, idn)
                        tdone = P1.do("tensor", fn, ts if tc == 3 else None)
                    if cc == 7:
                        trans_done[t] = tdone
                    P1.w("vector", ts, tdone)
                    if t >= 1 and cc == 0:
                        P1.w("vector", g_xn, 16 * t)
                        P1.w("vector", ts, qproj_done[t - 1])
                    ev = P1.do("vector", lambda e, o=xnT[:, 512 * cc:512 * cc + 512],
                               i=pC[:, 1024 * half:1024 * half + 512]:
                               e.tensor_copy(o, i), vs)
                    evac_guard[half] = ev
                    xsl = xnT[:, 512 * cc:512 * cc + 512]
                    P1.do("vector", lambda e, o=dbufm[:, 1:512], a=xsl[:, 1:512],
                          b=xsl[:, 0:511]: e.tensor_sub(o, a, b), vs)
                    P1.do("vector", lambda e, o=dbufm[:, 0:1], a=xsl[:, 0:1],
                          b=carry[:, cc:cc + 1]: e.tensor_sub(o, a, b), vs)
                    P1.do("vector", lambda e, o=xrT[:, 512 * cc + 1:512 * cc + 512],
                          i=dbufm[:, 1:512], m=mur[:, cc:cc + 1], b=xsl[:, 0:511]:
                          e.scalar_tensor_tensor(o, i, m, b, ALU.mult, ALU.add), vs)
                    P1.do("vector", lambda e, o=xrT[:, 512 * cc:512 * cc + 1],
                          i=dbufm[:, 0:1], m=mur[:, cc:cc + 1], b=carry[:, cc:cc + 1]:
                          e.scalar_tensor_tensor(o, i, m, b, ALU.mult, ALU.add), vs)
                    P1.do("vector", lambda e, o=xkT[:, 512 * cc + 1:512 * cc + 512],
                          i=dbufm[:, 1:512], m=muk[:, cc:cc + 1], b=xsl[:, 0:511]:
                          e.scalar_tensor_tensor(o, i, m, b, ALU.mult, ALU.add), vs)
                    md = P1.do("vector", lambda e, o=xkT[:, 512 * cc:512 * cc + 1],
                               i=dbufm[:, 0:1], m=muk[:, cc:cc + 1],
                               b=carry[:, cc:cc + 1]:
                               e.scalar_tensor_tensor(o, i, m, b, ALU.mult, ALU.add),
                               vs)
                    mix_done[cc] = md
                mix_last = P1.do(
                    "vector", lambda e, o=carry[:],
                    i=xnT[:].rearrange("p (c u) -> p c u", u=512)[:, :, 511:512]:
                    e.tensor_copy(o, i), vs)
                P1.w("sync", vs, mix_last)
                P1.do("sync", lambda e,
                      d=xnT_scr.rearrange("(cc p) u -> p cc u", p=128)[:, :, TT * t:TT * t + TT],
                      s=xnT[:].rearrange("p (cc u) -> p cc u", u=512):
                      e.dma_start(d, s), g_xn, 16)

                if t == 0:
                    P1.w("tensor", vs, wready["wrT"])
                for tc in range(4):
                    slot = tc % 2
                    gs, gv = pA_guard[slot]
                    P1.w("tensor", gs, gv)
                    rp = 0
                    for cin in range(8):
                        if tc == 0:
                            P1.w("tensor", vs, mix_done[cin])
                        for h in range(2):
                            fn = lambda e, o=pA[:, 1024 * slot + 512 * h:
                                               1024 * slot + 512 * h + 512], \
                                        l=xrT[:, 512 * cin + 128 * tc:
                                              512 * cin + 128 * tc + 128], \
                                        r=wrTs[:, 1024 * cin + 512 * h:
                                               1024 * cin + 512 * h + 512], \
                                        st=(cin == 0), sp=(cin == 7): \
                                e.matmul(o, l, r, start=st, stop=sp)
                            rp = P1.do("tensor", fn,
                                       ts if (cin == 7 and h == 1) else None)
                    P1.w("scalar", ts, rp)
                    if tc == 0:
                        P1.w("scalar", vs, last_ffn_vs)
                    else:
                        P1.w("scalar", vs, rsig_done)
                    rexp = P1.do("scalar", lambda e, o=scr4[:],
                                 i=pA[:, 1024 * slot:1024 * slot + 1024]:
                                 e.activation(o, i, AF.Exp, scale=-1.0), ss)
                    pA_guard[slot] = (ss, rexp)
                    P1.w("vector", ss, rexp)
                    P1.do("vector", lambda e, o=scr4[:]:
                          e.tensor_scalar(o, o, 1.0, None, ALU.add), vs)
                    rsig_done = P1.do(
                        "vector", lambda e, o=rstore[:, 1024 * tc:1024 * tc + 1024],
                        i=scr4[:]: e.reciprocal(o, i), vs)

                if t == 0:
                    P1.w("tensor", vs, wready["wkfT"])
                for co in range(8):
                    slot = co % 2
                    gs, gv = pB_guard[slot]
                    P1.w("tensor", gs, gv)
                    kp = 0
                    for cin in range(8):
                        fn = lambda e, o=pB[:, 512 * slot:512 * slot + 512], \
                                    l=wkfTs[:, 1024 * cin + 128 * co:
                                            1024 * cin + 128 * co + 128], \
                                    r=xkT[:, 512 * cin:512 * cin + 512], \
                                    st=(cin == 0), sp=(cin == 7): \
                            e.matmul(o, l, r, start=st, stop=sp)
                        kp = P1.do("tensor", fn, ts if cin == 7 else None)
                    P1.w("vector", ts, kp)
                    P1.w("vector", ss, rexp)
                    P1.do("vector", lambda e, o=scr4[:, 0:512],
                          i=pB[:, 512 * slot:512 * slot + 512]:
                          e.tensor_scalar(o, i, 0.0, None, ALU.max), vs)
                    ke = P1.do("vector", lambda e,
                               o=kactT[:, 512 * co:512 * co + 512],
                               a=scr4[:, 0:512]: e.tensor_mul(o, a, a), vs)
                    pB_guard[slot] = (vs, ke)

                if t == 0:
                    P1.w("tensor", vs, wready["wqT"])
                qp_last = 0
                for co in range(8):
                    slot = co % 2
                    gs, gv = pB_guard[slot]
                    P1.w("tensor", gs, gv)
                    qp = 0
                    for cin in range(8):
                        fn = lambda e, o=pB[:, 512 * slot:512 * slot + 512], \
                                    l=wqTs[:, 1024 * cin + 128 * co:
                                           1024 * cin + 128 * co + 128], \
                                    r=xnT[:, 512 * cin:512 * cin + 512], \
                                    st=(cin == 0), sp=(cin == 7): \
                            e.matmul(o, l, r, start=st, stop=sp)
                        qp = P1.do("tensor", fn, ts if cin == 7 else None)
                    qp_last = qp
                    P1.w("vector", ts, qp)
                    P1.w("vector", g_q[slot], g_q_uses[slot])
                    qev = P1.do("vector", lambda e, o=qe[slot][:],
                                i=pB[:, 512 * slot:512 * slot + 512]:
                                e.tensor_copy(o, i), vs)
                    pB_guard[slot] = (vs, qev)
                    P1.w("sync", vs, qev)
                    P1.do("sync", lambda e,
                          d=qT_scr[128 * co:128 * co + 128, TT * t:TT * t + TT],
                          s=qe[slot][:]: e.dma_start(d, s), g_q[slot], 16)
                    g_q_uses[slot] = g_q[slot].n
                qproj_done[t] = qp_last

                if t == 0:
                    P1.w("tensor", vs, wready["wvfT"])
                for tc in range(4):
                    slot = tc % 2
                    gs, gv = pA_guard[slot]
                    P1.w("tensor", gs, gv)
                    if tc == 0:
                        P1.w("tensor", vs, ke)
                    vp = 0
                    for din in range(8):
                        for h in range(2):
                            fn = lambda e, o=pA[:, 1024 * slot + 512 * h:
                                               1024 * slot + 512 * h + 512], \
                                        l=kactT[:, 512 * din + 128 * tc:
                                                512 * din + 128 * tc + 128], \
                                        r=wvfTs[:, 1024 * din + 512 * h:
                                                1024 * din + 512 * h + 512], \
                                        st=(din == 0), sp=(din == 7): \
                                e.matmul(o, l, r, start=st, stop=sp)
                            vp = P1.do("tensor", fn,
                                       ts if (din == 7 and h == 1) else None)
                    P1.w("vector", ts, vp)
                    P1.w("vector", ss, xn_done)
                    P1.do("vector", lambda e, o=scr4[:],
                          a=pA[:, 1024 * slot:1024 * slot + 1024],
                          b=rstore[:, 1024 * tc:1024 * tc + 1024]:
                          e.tensor_mul(o, a, b), vs)
                    ffn = P1.do("vector",
                                lambda e, o=xb[:, 1024 * tc:1024 * tc + 1024],
                                a=scr4[:], b=xb[:, 1024 * tc:1024 * tc + 1024]:
                                e.tensor_add(o, a, b), vs)
                    pA_guard[slot] = (vs, ffn)
                    last_ffn_vs = ffn
                P1.w("sync", vs, last_ffn_vs)
                P1.do("sync", lambda e,
                      d=part_scr[TT * t:TT * t + TT, :].rearrange("(c p) f -> p c f", p=128),
                      s=xb[:].rearrange("p (c f) -> p c f", c=4):
                      e.dma_start(d, s), g_po[t % 2], 16)
                if t + 2 < NT:
                    load_x(t + 2)

            ph1_ts, ph1_ss, ph1_vs = ts.n, ss.n, vs.n
            P1.emit(nc)

        P2 = Plan()
        with ExitStack() as ph2:
            sb2 = lambda n, sh, dt: ph2.enter_context(nc.sbuf_tensor(n, sh, dt))
            wkaTs = sb2("wkaTs", [128, 8192], BF16)
            wvaTs = sb2("wvaTs", [128, 8192], BF16)
            wos = sb2("wos", [128, 8192], BF16)
            Kst = sb2("Kst", [128, 32768], BF16)
            Vst = sb2("Vst", [128, NB * 1040], BF16)
            xn2 = sb2("xn2", [128, 4096], BF16)
            qt = sb2("qt", [128, 4096], BF16)
            ptb = sb2("ptb", [128, 2048], BF16)
            yt = sb2("yt", [128, 4096], BF16)
            pbuf = [sb2(f"pbuf{i}", [128, 512], F32) for i in range(3)]
            stg2 = [sb2("stg2_0", [128, 1024], F32)]
            linv = stg2[0][0:1, :]

            pS = ph2.enter_context(nc.psum_tensor("pS", [128, 2048], F32))
            pO = ph2.enter_context(nc.psum_tensor("pO", [128, 2048], F32))

            for e in ENGINES:
                P2.w(e, ts, ph1_ts)
                P2.w(e, ss, ph1_ss)
                P2.w(e, vs, ph1_vs)
                for s2 in g_po + g_q + [g_xn]:
                    P2.w(e, s2, s2.n)

            wdst2 = {"wkaT": wkaTs, "wvaT": wvaTs, "wo": wos}
            stg2_uses = [0, 0]
            w2ready = 0
            for name in ("wkaT", "wvaT", "wo"):
                for cc in range(8):
                    P2.w("sync", vs, w2ready)
                    P2.do("sync", lambda e, d=stg2[0][:],
                          s=wsrc[name][128 * cc:128 * cc + 128, :]:
                          e.dma_start(d, s), d_wm[name], 16)
                    P2.w("vector", d_wm[name], d_wm[name].n)
                    w2ready = P2.do(
                        "vector",
                        lambda e, d=wdst2[name][:, 1024 * cc:1024 * cc + 1024],
                        s=stg2[0][:]: e.tensor_copy(d, s), vs)
            vones = P2.do("vector", lambda e,
                          o=Vst[:].rearrange("p (x c) -> p x c", c=VW)[:, :, 64:65]:
                          e.memset(o, 1.0), vs)

            pS_guard = [(ss, 0), (ss, 0)]
            pO_vguard = {0: 0, 1: 0}
            vproj_done = {}
            qk_last = {}
            addc_prev = 0
            rr = [0]
            out_uses = [0, 0, 0]
            pidx = [0]

            def load_xn2(g):
                P2.do("sync", lambda e, d=xn2[:].rearrange("p (cc u) -> p cc u", u=512),
                      s=xnT_scr.rearrange("(cc p) u -> p cc u", p=128)[:, :, TT * g:TT * g + TT]:
                      e.dma_start(d, s), d_xn2, 16)

            def load_qt(g):
                P2.do("sync", lambda e, d=qt[:].rearrange("p (cc u) -> p cc u", u=512),
                      s=qT_scr.rearrange("(cc p) u -> p cc u", p=128)[:, :, TT * g:TT * g + TT]:
                      e.dma_start(d, s), d_qt, 16)

            load_xn2(0)
            load_qt(0)
            for g in range(NT):
                P2.w("tensor", d_xn2, 16 * (g + 1))
                if g == 0:
                    P2.w("tensor", vs, w2ready)
                for pr in range(8):
                    half = pr % 2
                    gs, gv = pS_guard[half]
                    P2.w("tensor", gs, gv)
                    kp = 0
                    for cin in range(8):
                        fn = lambda e, o=pS[:, 1024 * half:1024 * half + 512], \
                                    l=wkaTs[:, 1024 * cin + 128 * pr:
                                            1024 * cin + 128 * pr + 128], \
                                    r=xn2[:, 512 * cin:512 * cin + 512], \
                                    st=(cin == 0), sp=(cin == 7): \
                            e.matmul(o, l, r, start=st, stop=sp)
                        kp = P2.do("tensor", fn, ts if cin == 7 else None)
                    P2.w("vector", ts, kp)
                    kev = P2.do("vector", lambda e,
                                o=Kst[:, 4096 * pr + TT * g:4096 * pr + TT * g + 512],
                                i=pS[:, 1024 * half:1024 * half + 512]:
                                e.tensor_copy(o, i), vs)
                    pS_guard[half] = (vs, kev)
                vev = 0
                if g >= 1:
                    P2.w("tensor", vs, addc_prev)
                for tkc in range(4):
                    half = tkc % 2
                    P2.w("tensor", vs, pO_vguard[half])
                    vp = 0
                    for cin in range(8):
                        for h in range(2):
                            fn = lambda e, o=pO[:, 1024 * half + 512 * h:
                                               1024 * half + 512 * h + 512], \
                                        l=xn2[:, 512 * cin + 128 * tkc:
                                              512 * cin + 128 * tkc + 128], \
                                        r=wvaTs[:, 1024 * cin + 512 * h:
                                                1024 * cin + 512 * h + 512], \
                                        st=(cin == 0), sp=(cin == 7): \
                                e.matmul(o, l, r, start=st, stop=sp)
                            vp = P2.do("tensor", fn,
                                       ts if (cin == 7 and h == 1) else None)
                    if tkc == 3:
                        vproj_done[g] = vp
                    P2.w("vector", ts, vp)
                    blk = 4 * g + tkc
                    vev = P2.do("vector", lambda e,
                                o=Vst[:, 1040 * blk:1040 * blk + 1040].rearrange(
                                    "p (h c) -> p h c", c=VW)[:, :, 0:64],
                                i=pO[:, 1024 * half:1024 * half + 1024].rearrange(
                                    "p (h c) -> p h c", c=64):
                                e.tensor_copy(o, i), vs)
                    pO_vguard[half] = vev
                kv_ev = vev
                if g + 1 < NT:
                    P2.w("sync", ts, vproj_done[g])
                    load_xn2(g + 1)

                J = 4 * g + 4
                P2.w("tensor", d_qt, 16 * (g + 1))
                P2.w("tensor", vs, kv_ev)
                qk_last_g = 0
                for p in range(8):
                    obase = 1024 * (p % 2)
                    expc = {}
                    qkc = {}
                    slot_of = {}

                    def emit_pv(j, Jv=J, pv=p, ob=obase, ec=expc,
                                so=slot_of):
                        P2.w("tensor", ss, ec[j])
                        if j == 0:
                            P2.w("tensor", vs, pO_vguard[pv % 2])
                        sl = so[j]
                        P2.do("tensor", lambda e,
                              o=pO[0:65, ob:ob + 512],
                              l=Vst[:, 1040 * j + VW * 2 * pv:
                                    1040 * j + VW * 2 * pv + VW],
                              r=ptb[:, 1024 * sl:1024 * sl + 512],
                              st=(j == 0), sp=(j == Jv - 1):
                              e.matmul(o, l, r, start=st, stop=sp,
                                       skip_group_check=True), None)
                        return P2.do("tensor", lambda e,
                                     o=pO[0:65, ob + 512:ob + 1024],
                                     l=Vst[:, 1040 * j + VW * (2 * pv + 1):
                                           1040 * j + VW * (2 * pv + 1) + VW],
                                     r=ptb[:, 1024 * sl + 512:1024 * sl + 1024],
                                     st=(j == 0), sp=(j == Jv - 1):
                                     e.matmul(o, l, r, start=st, stop=sp,
                                              skip_group_check=True),
                                     ts if j == Jv - 1 else None)

                    for j in range(J):
                        sl = rr[0] % 2
                        rr[0] += 1
                        slot_of[j] = sl
                        diag = j >= 4 * g
                        gs, gv = pS_guard[sl]
                        P2.w("tensor", gs, gv)
                        P2.do("tensor", lambda e,
                              o=pS[:, 1024 * sl:1024 * sl + 512],
                              l=Kst[0:64, 4096 * p + 128 * j:4096 * p + 128 * j + 128],
                              r=qt[0:64, 512 * p:512 * p + 512], sp=(not diag):
                              e.matmul(o, l, r, start=True, stop=sp,
                                       tile_position=(0, 0),
                                       skip_group_check=True), ts)
                        qkc[j] = P2.do("tensor", lambda e,
                                       o=pS[:, 1024 * sl + 512:1024 * sl + 1024],
                                       l=Kst[64:128, 4096 * p + 128 * j:
                                             4096 * p + 128 * j + 128],
                                       r=qt[64:128, 512 * p:512 * p + 512],
                                       sp=(not diag):
                                       e.matmul(o, l, r, start=True, stop=sp,
                                                tile_position=(64, 0),
                                                skip_group_check=True), ts)
                        if diag:
                            P2.do("tensor", lambda e,
                                  o=pS[:, 1024 * sl:1024 * sl + 512],
                                  l=ident[:],
                                  r=maskd[:, 512 * (j - 4 * g):
                                          512 * (j - 4 * g) + 512]:
                                  e.matmul(o, l, r, start=False, stop=True,
                                           skip_group_check=True), ts)
                            qkc[j] = P2.do("tensor", lambda e,
                                           o=pS[:, 1024 * sl + 512:1024 * sl + 1024],
                                           l=ident[:],
                                           r=maskd[:, 512 * (j - 4 * g):
                                                   512 * (j - 4 * g) + 512]:
                                           e.matmul(o, l, r, start=False, stop=True,
                                                    skip_group_check=True), ts)
                        P2.w("scalar", ts, qkc[j])
                        expc[j] = P2.do("scalar", lambda e,
                                        o=ptb[:, 1024 * sl:1024 * sl + 1024],
                                        i=pS[:, 1024 * sl:1024 * sl + 1024]:
                                        e.activation(o, i, AF.Exp, scale=0.125), ss)
                        pS_guard[sl] = (ss, expc[j])
                        if j >= 1:
                            emit_pv(j - 1)
                    pvlast = emit_pv(J - 1)
                    qk_last_g = qkc[J - 1]

                    P2.w("vector", ts, pvlast)
                    P2.do("vector", lambda e, o=yt[0:64, 512 * p:512 * p + 512],
                          i=pO[0:64, obase:obase + 512]: e.tensor_copy(o, i), vs)
                    P2.do("vector", lambda e, o=yt[64:128, 512 * p:512 * p + 512],
                          i=pO[0:64, obase + 512:obase + 1024]:
                          e.tensor_copy(o, i), vs)
                    P2.do("vector", lambda e, o=linv[0:1, 0:512],
                          i=pO[64:65, obase:obase + 512]: e.reciprocal(o, i), vs)
                    rec0 = P2.do("vector", lambda e, o=linv[0:1, 512:1024],
                                 i=pO[64:65, obase + 512:obase + 1024]:
                                 e.reciprocal(o, i), vs)
                    P2.w("vector", vs, rec0)
                    rec = P2.do("vector", lambda e, o=pv_scr[0:1, 0:4],
                                i=linv[0:1, 1020:1024]: e.tensor_copy(o, i), vs)
                    sl2 = rr[0] % 2
                    rr[0] += 1
                    gs, gv = pS_guard[sl2]
                    P2.w("tensor", gs, gv)
                    P2.w("tensor", vs, rec)
                    P2.do("tensor", lambda e,
                          o=pS[0:64, 1024 * sl2:1024 * sl2 + 512],
                          l=ones64[0:1, :], r=linv[0:1, 0:512]:
                          e.matmul(o, l, r, start=True, stop=True,
                                   skip_group_check=True), ts)
                    r1 = P2.do("tensor", lambda e,
                               o=pS[64:128, 1024 * sl2 + 512:1024 * sl2 + 1024],
                               l=ones64[0:1, :], r=linv[0:1, 512:1024]:
                               e.matmul(o, l, r, start=True, stop=True,
                                        tile_position=(0, 64),
                                        skip_group_check=True), ts)
                    P2.w("vector", ts, r1)
                    P2.do("vector", lambda e, o=yt[0:64, 512 * p:512 * p + 512],
                          b=pS[0:64, 1024 * sl2:1024 * sl2 + 512]:
                          e.tensor_mul(o, o, b), vs)
                    yev = P2.do("vector", lambda e,
                                o=yt[64:128, 512 * p:512 * p + 512],
                                b=pS[64:128, 1024 * sl2 + 512:1024 * sl2 + 1024]:
                                e.tensor_mul(o, o, b), vs)
                    pS_guard[sl2] = (vs, yev)
                    pO_vguard[p % 2] = yev
                qk_last[g] = qk_last_g
                if g + 1 < NT:
                    P2.w("sync", ts, qk_last[g])
                    load_qt(g + 1)

                if g == 0:
                    P2.w("tensor", d_wm["wo"], d_wm["wo"].n)
                P2.w("tensor", vs, pO_vguard[0])
                P2.w("tensor", vs, pO_vguard[1])
                addc = {}
                for tqc in range(4):
                    off = 1024 * (tqc % 2)
                    if tqc >= 2:
                        P2.w("tensor", vs, addc[tqc - 2])
                    op = 0
                    for hp in range(8):
                        for h in range(2):
                            fn = lambda e, o=pO[:, off + 512 * h:off + 512 * h + 512], \
                                        l=yt[:, 512 * hp + 128 * tqc:
                                             512 * hp + 128 * tqc + 128], \
                                        r=wos[:, 1024 * hp + 512 * h:
                                              1024 * hp + 512 * h + 512], \
                                        st=(hp == 0), sp=(hp == 7): \
                                e.matmul(o, l, r, start=st, stop=sp,
                                         skip_group_check=True)
                            op = P2.do("tensor", fn,
                                       ts if (hp == 7 and h == 1) else None)
                    for h in range(2):
                        i = pidx[0]
                        pidx[0] += 1
                        sl = i % 3
                        P2.w("sync", g_out[sl], out_uses[sl])
                        row = TT * g + 128 * tqc
                        P2.do("sync", lambda e, d=pbuf[sl][:],
                              s=part_scr[row:row + 128, 512 * h:512 * h + 512]:
                              e.dma_start(d, s), d_pt[sl], 16)
                        pt_val = d_pt[sl].n
                        P2.w("vector", ts, op)
                        P2.w("vector", d_pt[sl], pt_val)
                        ad = P2.do("vector", lambda e, o=pbuf[sl][:],
                                   a=pO[:, off + 512 * h:off + 512 * h + 512],
                                   b=pbuf[sl][:]: e.tensor_add(o, a, b), vs)
                        addc[tqc] = ad
                        P2.w("sync", vs, ad)
                        P2.do("sync", lambda e,
                              d=out_h[row:row + 128, 512 * h:512 * h + 512],
                              s=pbuf[sl][:]: e.dma_start(d, s), g_out[sl], 16)
                        out_uses[sl] = g_out[sl].n
                addc_prev = addc[3]

            for sl in range(3):
                P2.w("sync", g_out[sl], g_out[sl].n)

            P2.emit(nc)

    return nc


_NC_CACHE = None


def kernel(L0_attn_x_q, L0_attn_w_q, L0_attn_w_k, L0_attn_w_v, L0_attn_w_o,
           L0_ffn_W_r, L0_ffn_W_k, L0_ffn_W_v, L0_ffn_mu_r, L0_ffn_mu_k):
    global _NC_CACHE
    x = np.ascontiguousarray(np.asarray(L0_attn_x_q, dtype=np.float32))
    bf = ml_dtypes.bfloat16

    def tr(w):
        return np.ascontiguousarray(
            np.asarray(w, dtype=np.float32).reshape(CDIM, CDIM).T)

    common = dict(
        wqT=tr(L0_attn_w_q), wkaT=tr(L0_attn_w_k), wvaT=tr(L0_attn_w_v),
        wo=np.ascontiguousarray(np.asarray(L0_attn_w_o, dtype=np.float32)),
        wrT=tr(L0_ffn_W_r), wkfT=tr(L0_ffn_W_k), wvfT=tr(L0_ffn_W_v),
        mur=np.ascontiguousarray(
            np.asarray(L0_ffn_mu_r, dtype=np.float32).reshape(8, 128).T),
        muk=np.ascontiguousarray(
            np.asarray(L0_ffn_mu_k, dtype=np.float32).reshape(8, 128).T),
        ident=np.eye(128, dtype=bf),
        maskd=np.concatenate(
            [np.where(np.arange(512)[None, :] < (np.arange(128)[:, None] + 128 * d),
                      np.float32(-30000.0), np.float32(0.0))
             for d in range(4)], axis=1).astype(bf),
        ones64=np.ones((1, 64), dtype=np.float32),
    )
    if _NC_CACHE is None:
        _NC_CACHE = build_nc()
    nc = _NC_CACHE
    in_maps = [dict(common, x=np.ascontiguousarray(x[i])) for i in range(8)]
    res = run_bass_kernel_spmd(nc, in_maps, core_ids=list(range(8)))
    return np.stack([r["out"] for r in res.results], axis=0)


if __name__ == "__main__":
    import reference as refmod
    inp = {k: np.asarray(v) for k, v in refmod.setup_inputs().items()}
    got = kernel(**inp)
    want = np.asarray(refmod.reference(**inp))
    denom = float(np.square(want).mean())
    print("resvar", float(np.square(got - want).mean() / denom),
          "maxabs", float(np.abs(got - want).max()))
